# version 12
# baseline (speedup 1.0000x reference)
import numpy as np
import concourse.bass as bass
import concourse.mybir as mybir
from concourse.bass_utils import run_bass_kernel_spmd

F32 = mybir.dt.float32
BF16 = mybir.dt.bfloat16
AF = mybir.ActivationFunctionType
ALU = mybir.AluOpType

D = 2048
NK = 16
NE = 66
T = 2048
NG = 4
NC = 8
CH = 256
NH = 32
EPS = 1e-5

ETILE_ORDER = [65, 64] + list(range(32, 64)) + list(range(0, 32))


def _build():
    nc = bass.Bass("TRN2", target_bir_lowering=False)

    d_uT = nc.dram_tensor("uT", [D, T], F32, kind="ExternalInput")
    d_win = nc.dram_tensor("w_in", [NE, 128, NK, 128], F32, kind="ExternalInput")
    d_wout = nc.dram_tensor("w_out", [16, 128, 32, 128], F32, kind="ExternalInput")
    d_wffn = nc.dram_tensor("w_ffn", [16, 128, 16, 128], F32, kind="ExternalInput")
    d_cw = nc.dram_tensor("conv_w", [128, 33, 4], F32, kind="ExternalInput")
    d_cb = nc.dram_tensor("conv_b", [128, 33], F32, kind="ExternalInput")
    d_dtb = nc.dram_tensor("dt_bias", [32, 1], F32, kind="ExternalInput")
    d_alog = nc.dram_tensor("A_log", [32, 1], F32, kind="ExternalInput")
    d_dv = nc.dram_tensor("Dvec", [1, 32], F32, kind="ExternalInput")
    d_nw = nc.dram_tensor("norm_w", [128, 32], F32, kind="ExternalInput")
    d_fb = nc.dram_tensor("ffn_bias", [128, 16], F32, kind="ExternalInput")
    d_ones_r = nc.dram_tensor("ones_r", [1, 128], F32, kind="ExternalInput")
    d_ones_c = nc.dram_tensor("ones_c", [128, 1], F32, kind="ExternalInput")
    d_ident = nc.dram_tensor("ident", [128, 128], F32, kind="ExternalInput")
    d_trilm = nc.dram_tensor("trilm_d", [2, 128, 256], F32, kind="ExternalInput")
    d_flag = nc.dram_tensor("flagv", [1, 4], F32, kind="ExternalInput")

    d_xtm = nc.dram_tensor("x_tm", [32, 128, T], BF16, kind="Internal")
    d_xcm = nc.dram_tensor("x_cm", [NC, 128, NH, CH], BF16, kind="Internal")
    d_zcm = nc.dram_tensor("z_cm", [NC, 128, NH, CH], BF16, kind="Internal")
    d_yns = nc.dram_tensor("yn_s", [NC, 128, NH, CH], BF16, kind="Internal")
    d_x1 = nc.dram_tensor("x1_s", [16, 128, T], F32, kind="Internal")
    d_cumd = nc.dram_tensor("cum_d", [32, T], F32, kind="Internal")
    d_ecumd = nc.dram_tensor("ecum_d", [32, T], BF16, kind="Internal")
    d_ecumgd = nc.dram_tensor("ecumg_d", [32, T], BF16, kind="Internal")
    d_prevd = nc.dram_tensor("prev_d", [NC, 64, NH * 128], BF16, kind="Internal")
    d_rows = nc.dram_tensor("rows_d", [4, T], F32, kind="Internal")
    d_csp = nc.dram_tensor("c_split", [64, T], BF16, kind="Internal")
    d_cdm = nc.dram_tensor("cd_m", [32, 64], F32, kind="Internal")
    d_cc1 = nc.dram_tensor("cc1_in", [128, 99], F32, kind="Internal")
    d_cc1o = nc.dram_tensor("cc1_out", [256, 99], F32, kind="Internal")
    d_cc2 = nc.dram_tensor("cc2_in", [64, NH * 128], BF16, kind="Internal")
    d_cc2o = nc.dram_tensor("cc2_out", [128, NH * 128], BF16, kind="Internal")

    d_out = nc.dram_tensor("out_dT", [D, T], F32, kind="ExternalOutput")

    ctx = []

    def sb(name, shape, dt):
        t = nc.sbuf_tensor(name, shape, dt)
        ctx.append(t)
        return t.__enter__()

    def psum(name, shape, dt):
        t = nc.psum_tensor(name, shape, dt)
        ctx.append(t)
        return t.__enter__()

    def mksem(name):
        t = nc.semaphore(name)
        ctx.append(t)
        return t.__enter__()

    BIG1 = sb("BIG1", [128, 32768], BF16)
    BIG2 = sb("BIG2", [128, 32768], BF16)
    BIG3 = sb("BIG3", [128, 16384], BF16)
    onesr = sb("onesr", [1, 128], BF16)
    onesc = sb("onesc", [128, 1], BF16)
    identb = sb("identb", [128, 128], BF16)
    trilm = sb("trilm", [128, 512], BF16)
    cwb = sb("cwb", [128, 132], F32)
    cbb = sb("cbb", [128, 33], F32)
    dtbias = sb("dtbias", [32, 1], F32)
    Aneg = sb("Aneg", [32, 1], F32)
    Dbc = sb("Dbc", [128, 32], F32)
    nwb = sb("nwb", [128, 32], F32)
    fbb = sb("fbb", [128, 16], F32)
    alf = sb("alf", [128, 16], F32)
    flagc = sb("flagc", [128, 1], F32)
    epsc = sb("epsc", [1, 4], F32)
    oneb32 = sb("oneb32", [32, 1], F32)
    tinyb = sb("tinyb", [1, 64], BF16)
    ncumT = sb("ncumT", [128, 512], F32)
    dtTb = sb("dtTb", [128, 512], BF16)
    decT = sb("decT", [128, 512], F32)
    cdrows = sb("cdrows", [32, 64], F32)
    cdtr = sb("cdtr", [32, 64], F32)
    cdrow0 = sb("cdrow0", [1, 512], BF16)
    cdsb = sb("cdsb", [64, 32], F32)

    rinvr = sb("rinvr", [1, 256], F32)
    halos = sb("halos", [128, 99], BF16)
    halor = sb("halor", [128, 99], BF16)
    keep3 = sb("keep3", [128, 99], BF16)
    fixw = sb("fixw", [128, 99], F32)
    fixw2 = sb("fixw2", [128, 99], F32)
    fixb = sb("fixb", [128, 99], BF16)
    fixraw = sb("fixraw", [128, 198], F32)
    bcs = sb("bcs", [128, 2048], BF16)
    bcsC = sb("bcsC", [128, 2048], BF16)
    BCt = sb("BCt", [128, 1024], BF16)
    Bdecb = sb("Bdecb", [128, 1024], BF16)
    DIt = sb("DIt", [128, 512], BF16)
    Cxp = sb("Cxp", [128, 2048], BF16)
    wwb = sb("wwb", [128, 8192], BF16)

    PB0 = psum("PB0", [128, 512], F32)
    PB1 = psum("PB1", [128, 512], F32)
    PG = psum("PG", [128, 512], F32)
    PQ = psum("PQ", [128, 512], F32)
    PS = psum("PS", [128, 512], F32)
    PV = psum("PV", [128, 512], F32)
    PU = psum("PU", [128, 512], F32)
    PT = psum("PT", [128, 1024], BF16)

    sT = mksem("sT"); sV = mksem("sV"); sC = mksem("sC"); sCC = mksem("sCC")
    dW0 = mksem("dW0"); dW1 = mksem("dW1")
    dU = mksem("dU"); dG = mksem("dG")
    dXT = mksem("dXT"); dEB = mksem("dEB"); dEC = mksem("dEC")
    dCK = mksem("dCK"); dYN = mksem("dYN"); dWO = mksem("dWO"); dUR = mksem("dUR")
    wrS = mksem("wrS"); wrA = mksem("wrA"); wrG = mksem("wrG"); outw = mksem("outw")

    def f32v(t, lo, n, p0=0, p1=128):
        return t[p0:p1, lo:lo + 2 * n].bitcast(F32)

    statr3 = f32v(BIG3, 0, 512, 0, 1)
    statr = f32v(BIG3, 1024, 512, 0, 1)
    statr2 = f32v(BIG3, 2048, 512, 0, 1)

    ub = lambda k: BIG1[:, k * 2048:(k + 1) * 2048]
    x1b = ub
    ynt = lambda h: BIG1[:, (h % 8) * 256:(h % 8) * 256 + 256]
    yntg = lambda g4: BIG1[:, (g4 % 2) * 1024:(g4 % 2) * 1024 + 1024]
    ygt = lambda h: BIG1[:, 2048 + (h % 4) * 256: 2048 + (h % 4) * 256 + 256]
    sqt = lambda h: BIG1[:, 3072 + (h % 4) * 256: 3072 + (h % 4) * 256 + 256]
    MhS = lambda G: [BIG1[:, 4096:6144], BIG1[:, 6144:8192], BIG3[:, 13312:15360]][G % 3]
    MhH = lambda G, i: MhS(G)[:, i * 512:(i + 1) * 512]
    GtsS = lambda c: BIG3[:, 12288 + (c % 2) * 512: 12288 + (c % 2) * 512 + 512]
    ecumb = lambda qq: BIG1[0:64, 8192 + (qq % 2) * 2048: 8192 + (qq % 2) * 2048 + 2048]
    ecumbG = lambda qq: BIG1[64:128, 8192 + (qq % 2) * 2048: 8192 + (qq % 2) * 2048 + 2048]
    ecumbF = lambda qq: BIG1[0:128, 8192 + (qq % 2) * 2048: 8192 + (qq % 2) * 2048 + 2048]
    prevb = lambda c: BIG1[0:64, 12288 + (c % 2) * 4096: 12288 + (c % 2) * 4096 + 4096]
    prevbF = lambda c: BIG1[0:128, 12288 + (c % 2) * 4096: 12288 + (c % 2) * 4096 + 4096]
    crecv = BIG3[0:64, 8192:12288]
    tmp64 = BIG3[0:64, 12288:12800]
    carry = f32v(BIG3, 4096, 4096, 0, 64)
    carryb = BIG3[0:64, 12288:16384]
    xdtT_m = lambda zt: BIG2[:, zt * 4096:(zt + 1) * 4096]
    xdtT_b = lambda zt: BIG1[:, 24576 + zt * 4096: 24576 + (zt + 1) * 4096]

    sqb = lambda k: BIG2[:, k * 512:(k + 1) * 512]
    ztst = lambda p: BIG2[:, 20488 + (p % 4) * 2048: 20488 + (p % 4) * 2048 + 2048]
    xsst = lambda p: BIG2[:, 8192 + (p % 4) * 2048: 8192 + (p % 4) * 2048 + 2048]
    xraw = lambda p: BIG2[:, 16384 + (p % 2) * 2052: 16384 + (p % 2) * 2052 + 2052]
    xck = lambda c: BIG2[:, (c % 2) * 8192:(c % 2) * 8192 + 8192]
    zck = lambda c: BIG2[:, 16384 + (c % 2) * 8192: 16384 + (c % 2) * 8192 + 8192]
    ynb = lambda j: BIG2[:, j * 1024:(j + 1) * 1024]
    swb = lambda o: BIG2[:, o * 2048:(o + 1) * 2048]

    QbA = BIG3[:, 0:2048]
    PbA = BIG3[:, 2048:4096]
    Ebc = lambda c, qq: f32v(BIG3, ((4 * c + qq) % 3) * 4096, 2048)
    rinvb = f32v(BIG3, 0, 2048)
    ureb = lambda p: f32v(BIG3, 4096 + (p % 2) * 2048, 1024)
    x1t = lambda g: f32v(BIG3, 12288 + (g % 2) * 1024, 512)
    QbD = BIG3[:, 0:2048]
    PbD = BIG3[:, 2048:4096]
    abst = BIG3[:, 4096:6144]
    d_ot = lambda g: f32v(BIG3, 6144 + (g % 2) * 1024, 512)
    sqd = lambda k: BIG3[:, 8192 + k * 512: 8192 + (k + 1) * 512]
    x1rb = lambda p: f32v(BIG3, 8192 + (p % 2) * 4096, 2048)

    rows_dt = f32v(BIG2, 0, 2048, 0, 32)
    rows_dA = f32v(BIG2, 4096, 2048, 0, 32)
    rows_cum = f32v(BIG3, 4096, 2048, 0, 32)
    rows_tmp = rows_dA
    fixT = BIG3[0:3, 0:4096]

    steps = []
    cnt = {}

    def em(eng, fn, waits=(), inc=None, amt=1):
        for (s, v) in waits:
            assert v <= cnt.get(id(s), 0), (eng, v, cnt.get(id(s), 0))
        steps.append((eng, fn, tuple(waits), inc, amt))
        if inc is not None:
            cnt[id(inc)] = cnt.get(id(inc), 0) + amt
        return cnt.get(id(inc)) if inc is not None else None

    def now(s):
        return cnt.get(id(s), 0)

    def dw(s):
        return (s, now(s))

    mm = {}

    def gl(out_ap, in_ap):
        em("gpsimd", lambda e, o=out_ap, i=in_ap: e.dma_start(out=o, in_=i), inc=dU, amt=16)

    gl(onesr[:, :], d_ones_r[:, :])
    gl(onesc[:, :], d_ones_c[:, :])
    gl(identb[:, :], d_ident[:, :])
    gl(trilm[:, 0:256], d_trilm[0])
    gl(trilm[:, 256:512], d_trilm[1])
    gl(cwb[:, 0:132], d_cw[:, :, :])
    gl(cbb[:, :], d_cb[:, :])
    gl(dtbias[:, :], d_dtb[:, :])
    gl(Aneg[:, :], d_alog[:, :])
    gl(nwb[:, :], d_nw[:, :])
    gl(fbb[:, :], d_fb[:, :])
    gl(tinyb[0:1, 0:1], d_flag[0:1, 0:1])
    gl(epsc[:, :], d_flag[:, :])
    gl(tinyb[0:1, 16:48], d_dv[:, :])
    for k in range(NK):
        gl(ub(k), d_uT[k * 128:(k + 1) * 128, :])

    em("vector", lambda e: e.memset(oneb32[:, :], 1.0), waits=[], inc=sV)
    em("scalar", lambda e: e.activation(Aneg[:, :], Aneg[:, :], AF.Exp),
       waits=[dw(dU)], inc=sC)
    em("vector", lambda e: e.tensor_scalar_mul(Aneg[:, :], Aneg[:, :], -1.0),
       waits=[(sC, now(sC))], inc=sV)
    em("tensor", lambda e: e.matmul(PU[:, 0:1], onesr[0:1, :], tinyb[0:1, 0:1],
                                    start=True, stop=True), waits=[dw(dU)], inc=None)
    em("tensor", lambda e: e.matmul(PU[:, 32:64], onesr[0:1, :], tinyb[0:1, 16:48],
                                    start=True, stop=True), waits=[], inc=sT)
    t0 = now(sT)
    em("vector", lambda e: e.tensor_copy(flagc[:, :], PU[:, 0:1]), waits=[(sT, t0)], inc=None)
    em("vector", lambda e: e.tensor_copy(Dbc[:, :], PU[:, 32:64]), waits=[], inc=sV)
    v_bc0 = now(sV)
    mm["pt_copy"] = 0
    mm["pu_free"] = v_bc0

    def emit_ln_stats(src_tile, sq_tile, first_waits):
        for g in range(NG):
            for k in range(NK):
                em("tensor", lambda e, k=k, g=g, src_tile=src_tile: e.matmul(
                    PG[0:1, 0:512], onesc[:, :], src_tile(k)[:, g * 512:(g + 1) * 512],
                    start=(k == 0), stop=(k == NK - 1)),
                    waits=(list(first_waits) if (g == 0 and k == 0)
                           else ([(sV, now(sV))] if k == 0 else [])),
                    inc=(sT if k == NK - 1 else None))
            em("vector", lambda e: e.tensor_scalar_mul(
                statr[0:1, 0:512], PG[0:1, 0:512], 1.0 / 2048.0),
                waits=[(sT, now(sT))], inc=sV)
            for k in range(NK):
                em("scalar", lambda e, k=k, g=g, src_tile=src_tile, sq_tile=sq_tile: e.activation(
                    sq_tile(k), src_tile(k)[:, g * 512:(g + 1) * 512], AF.Square),
                    waits=(([(sT, now(sT))] if g > 0 else list(first_waits)) if k == 0 else []),
                    inc=(sC if k == NK - 1 else None))
            for k in range(NK):
                em("tensor", lambda e, k=k, sq_tile=sq_tile: e.matmul(
                    PQ[0:1, 0:512], onesc[:, :], sq_tile(k),
                    start=(k == 0), stop=(k == NK - 1)),
                    waits=([(sC, now(sC))] if k == 0 else []),
                    inc=(sT if k == NK - 1 else None))
            em("vector", lambda e: e.tensor_scalar_mul(
                statr2[0:1, 0:512], PQ[0:1, 0:512], 1.0 / 2048.0),
                waits=[(sT, now(sT))], inc=sV)
            em("vector", lambda e: e.tensor_tensor(
                statr3[0:1, 0:512], statr[0:1, 0:512], statr[0:1, 0:512], ALU.mult),
                waits=[], inc=None)
            em("vector", lambda e: e.tensor_tensor(
                statr2[0:1, 0:512], statr2[0:1, 0:512], statr3[0:1, 0:512], ALU.subtract),
                waits=[], inc=sV)
            em("scalar", lambda e: e.activation(
                statr2[0:1, 0:512], statr2[0:1, 0:512], AF.Ln, bias=epsc[0:1, 1:2]),
                waits=[(sV, now(sV))], inc=None)
            em("scalar", lambda e: e.activation(
                statr2[0:1, 0:512], statr2[0:1, 0:512], AF.Exp, scale=-0.5),
                waits=[], inc=sC)
            em("vector", lambda e: e.tensor_tensor(
                statr3[0:1, 0:512], statr[0:1, 0:512], statr2[0:1, 0:512], ALU.mult),
                waits=[(sC, now(sC))], inc=sV)
            em("sync", lambda e, g=g: e.dma_start(
                out=d_rows[0:1, g * 512:(g + 1) * 512], in_=statr2[0:1, 0:512]),
                waits=[(sC, now(sC))], inc=wrS, amt=16)
            em("sync", lambda e, g=g: e.dma_start(
                out=d_rows[1:2, g * 512:(g + 1) * 512], in_=statr3[0:1, 0:512]),
                waits=[(sV, now(sV))], inc=wrS, amt=16)

    emit_ln_stats(ub, sqb, [dw(dU), (sV, v_bc0)])
    em("gpsimd", lambda e: e.dma_start(out=QbA, in_=d_rows[0:1, :].partition_broadcast(128)),
       waits=[dw(wrS)], inc=dG, amt=16)
    em("gpsimd", lambda e: e.dma_start(out=PbA, in_=d_rows[1:2, :].partition_broadcast(128)),
       waits=[], inc=dG, amt=16)
    q_qbpb = now(dG)
    for k in range(NK):
        em("vector", lambda e, k=k: e.tensor_tensor(ub(k), ub(k), QbA, ALU.mult),
           waits=([(dG, q_qbpb)] if k == 0 else []), inc=None)
        em("vector", lambda e, k=k: e.tensor_tensor(ub(k), ub(k), PbA, ALU.subtract),
           waits=[], inc=(sV if k == NK - 1 else None))
    v_uhat = now(sV)
    em("vector", lambda e: e.memset(xraw(0)[:, 0:3], 0.0), waits=[], inc=None)
    em("vector", lambda e: e.memset(xraw(1)[:, 0:3], 0.0), waits=[], inc=sV)

    wsem = [dW0, dW1]
    wslot = lambda p: wwb[:, (p % 2) * 2048:(p % 2) * 2048 + 2048]
    t_tile = {}
    pb_evac = {0: now(sV), 1: now(sV), 2: now(sV), 3: now(sV)}
    xs_free = {0: [], 1: [], 2: [], 3: []}
    z_free = {0: [], 1: [], 2: [], 3: []}
    raw_free = {0: now(sV), 1: now(sV)}

    def wload(i):
        p = i % 2
        waits = []
        if i >= 2:
            waits.append((sT, t_tile[ETILE_ORDER[i - 2]]))
        em("gpsimd", lambda e, ei=ETILE_ORDER[i], p=p: e.dma_start(out=wslot(p), in_=d_win[ei]),
           waits=waits, inc=wsem[p], amt=16)

    def inproj(i, extra0, evac_fn):
        p = i % 2
        e = ETILE_ORDER[i]
        for g in range(NG):
            pbuf = [PB0, PB1, PG, PQ][g]
            waits = [(sV, pb_evac[g])]
            if g == 0:
                waits.append((wsem[p], 16 * ((i // 2) + 1)))
                waits.extend(extra0)
            for k in range(NK):
                em("tensor", lambda en, p=p, k=k, g=g, pbuf=pbuf: en.matmul(
                    pbuf[:, :], wslot(p)[:, k * 128:(k + 1) * 128],
                    ub(k)[:, g * 512:(g + 1) * 512],
                    start=(k == 0), stop=(k == NK - 1)),
                    waits=(waits if k == 0 else []),
                    inc=(sT if k == NK - 1 else None))
            evac_fn(g, pbuf)
            pb_evac[g] = now(sV)
        t_tile[e] = now(sT)
        if i + 2 < NE:
            wload(i + 2)

    def emit_dt_machinery():
        em("scalar", lambda e: e.activation(rows_dt, rows_tmp, AF.Abs, bias=dtbias[:, 0:1]),
           waits=[(sV, now(sV))], inc=None)
        em("scalar", lambda e: e.activation(rows_dt, rows_dt, AF.Exp, scale=-1.0),
           waits=[], inc=None)
        em("scalar", lambda e: e.activation(rows_dt, rows_dt, AF.Ln, bias=oneb32[:, 0:1]),
           waits=[], inc=None)
        em("scalar", lambda e: e.activation(rows_cum, rows_tmp, AF.Relu, bias=dtbias[:, 0:1]),
           waits=[], inc=sC)
        em("vector", lambda e: e.tensor_tensor(rows_dt, rows_dt, rows_cum, ALU.add),
           waits=[(sC, now(sC))], inc=sV)
        em("vector", lambda e: e.tensor_scalar_mul(rows_dA, rows_dt, Aneg[:, 0:1]),
           waits=[], inc=sV)
        for c in range(NC):
            em("vector", lambda e, c=c: e.tensor_tensor_scan(
                rows_cum[:, c * 256:(c + 1) * 256], rows_dA[:, c * 256:(c + 1) * 256],
                rows_dA[:, c * 256:(c + 1) * 256], 0.0, ALU.add, ALU.bypass),
                waits=[], inc=(sV if c == NC - 1 else None))
        v_cum = now(sV)
        em("sync", lambda e: e.dma_start(out=d_cumd[:, :], in_=rows_cum),
           waits=[(sV, v_cum)], inc=wrS, amt=16)
        for c in range(NC):
            for zt in range(2):
                for bi in range(4):
                    col = c * 256 + zt * 128 + 32 * bi
                    dcol = (c * 2 + zt) * 32
                    em("vector", lambda e, col=col, dcol=dcol, bi=bi: e.transpose(
                        decT[32 * bi:32 * bi + 32, dcol:dcol + 32], rows_dt[:, col:col + 32]),
                        waits=[], inc=None)
                    em("vector", lambda e, col=col, dcol=dcol, bi=bi: e.transpose(
                        ncumT[32 * bi:32 * bi + 32, dcol:dcol + 32], rows_cum[:, col:col + 32]),
                        waits=[], inc=None)
        em("vector", lambda e: e.tensor_copy(dtTb[:, :], decT[:, :]), waits=[], inc=None)
        em("vector", lambda e: e.tensor_scalar_mul(ncumT[:, :], ncumT[:, :], -1.0),
           waits=[], inc=sV)
        em("scalar", lambda e: e.activation(rows_tmp, rows_cum, AF.Exp),
           waits=[(sV, now(sV))], inc=sC)
        em("gpsimd", lambda e: e.dma_start(out=d_ecumd[:, :], in_=rows_tmp),
           waits=[(sC, now(sC))], inc=wrG, amt=16)
        cum_ends = rows_cum.rearrange("p (c s) -> p c s", s=256)[:, :, 255]
        em("scalar", lambda e: e.activation(cdrows[:, 0:8], cum_ends, AF.Exp),
           waits=[(sV, now(sV))], inc=sC)
        em("vector", lambda e: e.tensor_tensor_scan(
            cdrows[:, 16:24], cdrows[:, 0:8], cdrows[:, 0:8], 1.0, ALU.mult, ALU.bypass),
            waits=[(sC, now(sC))], inc=None)
        em("vector", lambda e: e.tensor_scalar_mul(cdrows[:, 17:25], cdrows[:, 16:24], flagc[0:32, 0:1]),
           waits=[], inc=None)
        em("vector", lambda e: e.tensor_copy(cdrows[:, 16:17], flagc[0:32, 0:1]),
           waits=[], inc=None)
        em("vector", lambda e: e.transpose(cdtr[0:32, 0:32], cdrows[:, 0:32]),
           waits=[], inc=None)
        em("vector", lambda e: e.transpose(cdtr[0:32, 32:64], cdrows[:, 16:48]),
           waits=[], inc=sV)
        em("sync", lambda e: e.dma_start(out=d_cdm[:, :], in_=cdtr[:, :]),
           waits=[(sV, now(sV))], inc=wrS, amt=16)
        em("gpsimd", lambda e: e.dma_start(
            out=cdrow0[0:1, 0:256], in_=d_cdm[0:8, 0:32]), waits=[dw(wrS)], inc=dG, amt=16)
        em("gpsimd", lambda e: e.dma_start(
            out=cdrow0[0:1, 256:512], in_=d_cdm[0:8, 32:64]), waits=[], inc=dG, amt=16)
        mm["cdrow_q"] = now(dG)
        for c in range(NC):
            em("vector", lambda e, c=c: e.tensor_scalar_mul(
                rows_tmp[:, c * 256:(c + 1) * 256], rows_tmp[:, c * 256:(c + 1) * 256],
                cdrows[:, 16 + c:17 + c]),
                waits=([dw(wrG)] if c == 0 else []), inc=(sV if c == NC - 1 else None))
        em("gpsimd", lambda e: e.dma_start(out=d_ecumgd[:, :], in_=rows_tmp),
           waits=[(sV, now(sV))], inc=wrG, amt=16)
        for c in range(NC):
            em("vector", lambda e, c=c: e.tensor_scalar_sub(
                rows_tmp[:, c * 256:(c + 1) * 256], rows_cum[:, c * 256:(c + 1) * 256],
                rows_cum[:, c * 256 + 255:c * 256 + 256]),
                waits=([dw(wrG)] if c == 0 else []), inc=(sV if c == NC - 1 else None))
        em("scalar", lambda e: e.activation(rows_tmp, rows_tmp, AF.Exp, scale=-1.0),
           waits=[(sV, now(sV))], inc=sC)
        c_dec = now(sC)
        for c in range(NC):
            for zt in range(2):
                for bi in range(4):
                    col = c * 256 + zt * 128 + 32 * bi
                    dcol = (c * 2 + zt) * 32
                    em("vector", lambda e, col=col, dcol=dcol, bi=bi: e.transpose(
                        decT[32 * bi:32 * bi + 32, dcol:dcol + 32], rows_tmp[:, col:col + 32]),
                        waits=([(sC, c_dec)] if (c == 0 and zt == 0 and bi == 0) else []), inc=None)
        em("vector", lambda e: e.tensor_tensor(decT[:, :], decT[:, :], dtTb[:, :], ALU.mult),
           waits=[], inc=sV)
        mm["dt_done"] = now(sV)

    def emit_cc1_and_fix():
        em("gpsimd", lambda e: e.dma_start(out=d_cc1[:, :], in_=halos[:, :]),
           waits=[(sV, mm["halos_done"])], inc=wrG, amt=16)
        em("gpsimd", lambda e: e.collective_compute(
            "AllGather", ALU.bypass,
            replica_groups=[[0, 1], [2, 3], [4, 5], [6, 7]],
            ins=[d_cc1[:, :]], outs=[d_cc1o[:, :]]),
            waits=[dw(wrG)], inc=sCC)
        em("gpsimd", lambda e: e.dma_start(out=halor[:, :], in_=d_cc1o[0:128, :]),
           waits=[(sCC, now(sCC))], inc=dG, amt=16)
        q_h = now(dG)
        em("vector", lambda e: e.tensor_scalar_mul(halor[:, :], halor[:, :], flagc[:, 0:1]),
           waits=[(dG, q_h)], inc=sV)
        rv = fixraw.rearrange("p (j a) -> p j a", a=6)
        em("vector", lambda e: e.tensor_copy(
            rv[:, :, 0:3], halor[:, :].rearrange("p (j a) -> p j a", a=3)),
            waits=[(sV, now(sV))], inc=None)
        em("vector", lambda e: e.tensor_copy(
            rv[:, :, 3:6], keep3[:, :].rearrange("p (j a) -> p j a", a=3)),
            waits=[], inc=sV)
        v_rv = now(sV)
        cwv = cwb[:, 0:132].rearrange("p (j a) -> p j a", a=4)
        fwv = fixw.rearrange("p (j a) -> p j a", a=3)
        fw2 = fixw2.rearrange("p (j a) -> p j a", a=3)
        em("vector", lambda e: e.tensor_tensor(
            fwv, rv[:, :, 0:3], cwv[:, :, 0:1].broadcast_to([128, 33, 3]), ALU.mult),
            waits=[(sV, v_rv)], inc=None)
        for kk in range(1, 4):
            em("vector", lambda e, kk=kk: e.tensor_tensor(
                fw2, rv[:, :, kk:kk + 3], cwv[:, :, kk:kk + 1].broadcast_to([128, 33, 3]), ALU.mult),
                waits=[], inc=None)
            em("vector", lambda e: e.tensor_tensor(fwv, fwv, fw2, ALU.add), waits=[], inc=None)
        em("vector", lambda e: e.tensor_tensor(
            fwv, fwv, cbb[:, 0:33].unsqueeze(2).broadcast_to([128, 33, 3]), ALU.add),
            waits=[], inc=sV)
        em("scalar", lambda e: e.activation(fixb[:, :], fixw[:, :], AF.Silu),
           waits=[(sV, now(sV))], inc=sC)
        c_fix = now(sC)
        mm["fix_done"] = c_fix
        em("vector", lambda e: e.tensor_copy(bcs[:, 0:3], fixb[:, 96:99]),
           waits=[(sC, c_fix), (sV, mm["bcs_done"])], inc=sV)
        mm["bcs_fixed"] = now(sV)
        em("sync", lambda e: e.dma_start(out=d_csp[:, :], in_=bcs[64:128, :]),
           waits=[(sV, mm["bcs_fixed"])], inc=wrS, amt=16)
        em("gpsimd", lambda e: e.dma_start(out=bcsC[0:64, :], in_=d_csp[:, :]),
           waits=[dw(wrS)], inc=dG, amt=16)
        em("gpsimd", lambda e: e.dma_start(out=bcsC[64:128, :], in_=d_csp[:, :]),
           waits=[], inc=dG, amt=16)
        mm["bcsC_q"] = now(dG)
        for j in range(32):
            em("tensor", lambda e, j=j: e.transpose(
                PT[0:3, 0:128], fixb[:, j * 3:(j + 1) * 3], identb[:, :]),
                waits=([(sC, c_fix)] if j == 0 else []) + [(sV, mm["pt_copy"])],
                inc=sT)
            em("vector", lambda e, j=j: e.tensor_copy(
                fixT[0:3, j * 128:(j + 1) * 128], PT[0:3, 0:128]),
                waits=[(sT, now(sT))], inc=sV)
            mm["pt_copy"] = now(sV)
        mm["fixT_done"] = now(sV)
        em("vector", lambda e: e.memset(carry[:, :], 0.0), waits=[], inc=sV)
        mm["chain_prev"] = now(sV)

    Bdec_slot = lambda gp, i, zt: Bdecb[:, gp * 512 + i * 128 + zt * 64: gp * 512 + i * 128 + zt * 64 + 64]

    x2d = d_xtm[:, :, :].rearrange("j p t -> (j p) t")

    def load_xdtT(c, xv, freekey, extra=()):
        for zt in range(2):
            em("sync", lambda e, c=c, zt=zt, xv=xv: e.dma_start(
                out=xv(zt), in_=x2d[0:4096, (2 * c + zt) * 128:(2 * c + zt) * 128 + 128],
                transpose=True),
                waits=(list(extra) + [dw(wrS), (sV, mm[freekey])]) if zt == 0 else [],
                inc=dXT, amt=16)
        mm[(freekey, "q", c)] = now(dXT)
        if c == 0:
            em("vector", lambda e, xv=xv: e.tensor_copy(xv(0)[0:3, :], fixT[0:3, :]),
               waits=[(dXT, mm[(freekey, "q", c)]), (sV, mm["fixT_done"])], inc=sV)
            mm[(freekey, "fix", 0)] = now(sV)

    def emit_mini_chunk(c):
        if c == 0:
            emit_cc1_and_fix()
            mm["xdtT_m_free"] = now(sV)
        load_xdtT(c, xdtT_m, "xdtT_m_free")
        for zt in range(2):
            em("tensor", lambda e, c=c, zt=zt: e.transpose(
                PT[:, 0:128],
                bcs[:, c * 256 + zt * 128: c * 256 + zt * 128 + 128], identb[:, :]),
                waits=[(sV, mm["bcs_fixed"]), (sV, mm["pt_copy"])], inc=sT)
            mm["pt_t"] = now(sT)
            em("vector", lambda e, c=c, zt=zt: e.tensor_copy(
                BCt[:, (c * 2 + zt) * 64:(c * 2 + zt) * 64 + 64],
                PT[:, 0:64]),
                waits=[(sT, mm["pt_t"])], inc=sV)
            mm["pt_copy"] = now(sV)
        v_bct = now(sV)
        em("scalar", lambda e: e.activation(carryb[:, :], carry[:, :], AF.Copy),
           waits=[(sV, mm["chain_prev"]), (wrG, now(wrG))], inc=sC)
        c_prev = now(sC)
        em("gpsimd", lambda e, c=c: e.dma_start(out=d_prevd[c], in_=carryb[:, :]),
           waits=[(sC, c_prev)], inc=wrG, amt=16)
        em("tensor", lambda e, c=c: e.matmul(
            PU[0:64, 0:32], onesr[0:1, 0:64], cdrow0[0:1, c * 32:(c + 1) * 32],
            start=True, stop=True),
            waits=[(dG, mm["cdrow_q"]), (sV, mm["pu_free"])], inc=sT)
        em("vector", lambda e: e.tensor_copy(cdsb[:, :], PU[0:64, 0:32]),
           waits=[(sT, now(sT))], inc=sV)
        v_cd = now(sV)
        mm["pu_free"] = v_cd
        for g in range(8):
            heads = [4 * g + i for i in range(4)]
            pSt = [PS, PV][g % 2]
            for i, h in enumerate(heads):
                for zt in range(2):
                    blk = c * 2 + zt
                    em("vector", lambda e, h=h, zt=zt, blk=blk, i=i, g=g: e.tensor_scalar_mul(
                        Bdec_slot(g % 2, i, zt), BCt[:, blk * 64:blk * 64 + 64],
                        decT[:, blk * 32 + h:blk * 32 + h + 1]),
                        waits=(([(sV, v_bct)] +
                                ([(sT, mm[("st_mmg", c, g - 2)])] if g >= 2 else
                                 ([(sT, mm[("st_mmg", c - 1, 6 + g)])] if c > 0 else [])))
                               if (i == 0 and zt == 0) else []),
                        inc=(sV if (i == 3 and zt == 1) else None))
            v_bd = now(sV)
            for i, h in enumerate(heads):
                for zt in range(2):
                    em("tensor", lambda e, h=h, zt=zt, i=i, g=g, pSt=pSt: e.matmul(
                        pSt[0:64, i * 128:(i + 1) * 128], Bdec_slot(g % 2, i, zt),
                        xdtT_m(zt)[:, h * 128:(h + 1) * 128],
                        start=(zt == 0), stop=(zt == 1)),
                        waits=(([(sV, v_bd), (dXT, mm[("xdtT_m_free", "q", c)])] +
                                ([(sV, mm[("xdtT_m_free", "fix", 0)])] if c == 0 else []) +
                                ([(sV, mm[("chaing", c, g - 2)])] if g >= 2 else []) +
                                ([(sV, mm[("chaing", c - 1, 6 + g)])] if (c > 0 and g < 2) else []))
                               if (i == 0 and zt == 0) else []),
                        inc=(sT if (i == 3 and zt == 1) else None))
            mm[("st_mmg", c, g)] = now(sT)
            for i, h in enumerate(heads):
                em("vector", lambda e, h=h, i=i, pSt=pSt: e.scalar_tensor_tensor(
                    carry[:, h * 128:(h + 1) * 128], carry[:, h * 128:(h + 1) * 128],
                    cdsb[:, h:h + 1], pSt[0:64, i * 128:(i + 1) * 128], ALU.mult, ALU.add),
                    waits=([(sT, mm[("st_mmg", c, g)]), (sV, v_cd), (sC, c_prev)] if i == 0 else []),
                    inc=(sV if i == 3 else None))
            mm[("chaing", c, g)] = now(sV)
        mm["chain_prev"] = now(sV)
        mm["xdtT_m_free"] = now(sV)
        if c == NC - 1:
            em("scalar", lambda e: e.activation(carryb[:, :], carry[:, :], AF.Copy),
               waits=[(sV, mm["chain_prev"]), (wrG, now(wrG))], inc=sC)
            em("gpsimd", lambda e: e.dma_start(out=d_cc2[:, :], in_=carryb[:, :]),
               waits=[(sC, now(sC))], inc=wrG, amt=16)
            em("gpsimd", lambda e: e.collective_compute(
                "AllGather", ALU.bypass,
                replica_groups=[[0, 1], [2, 3], [4, 5], [6, 7]],
                ins=[d_cc2[:, :]], outs=[d_cc2o[:, :]]),
                waits=[dw(wrG)], inc=sCC)
            mm["cc2_done"] = now(sCC)

    for _k in range(4):
        pb_evac[_k] = now(sV)
    wload(0)
    wload(1)
    minis = 0
    for i, e in enumerate(ETILE_ORDER):
        role = ("dt" if e == 65 else "bc" if e == 64 else "x" if e >= 32 else "z")

        if role == "dt":
            def evac_dt(g, pbuf):
                tmm = now(sT)
                em("vector", lambda en, g=g, pbuf=pbuf: en.tensor_copy(
                    rows_tmp[:, g * 512:(g + 1) * 512], pbuf[0:32, :]),
                    waits=[(sT, tmm)], inc=sV)
            inproj(i, [(sV, v_uhat)], evac_dt)
            emit_dt_machinery()
            continue

        if role in ("bc", "x"):
            xb = i % 4
            dst = xraw(i % 2)
            j = 32 if role == "bc" else (e - 32)

            def evac_x(g, pbuf, dst=dst):
                tmm = now(sT)
                em("vector", lambda en, g=g, pbuf=pbuf, dst=dst: en.tensor_copy(
                    dst[:, 3 + g * 512: 3 + (g + 1) * 512], pbuf[:, :]),
                    waits=[(sT, tmm)], inc=sV)
            inproj(i, [(sV, raw_free[i % 2])], evac_x)
            em("vector", lambda en, dst=dst, j=j: en.tensor_copy(
                keep3[:, j * 3:(j + 1) * 3], dst[:, 3:6]), waits=[], inc=None)
            em("vector", lambda en, dst=dst, j=j: en.tensor_copy(
                halos[:, j * 3:(j + 1) * 3], dst[:, 2048:2051]), waits=[], inc=sV)
            if j == 31:
                mm["halos_done"] = now(sV)
            xs = xsst(xb)
            em("vector", lambda en, dst=dst, j=j, xs=xs: en.tensor_scalar(
                xs, dst[:, 0:2048], cwb[:, 4 * j:4 * j + 1], cbb[:, j:j + 1],
                ALU.mult, ALU.add), waits=list(xs_free[xb]), inc=None)
            for kk in range(1, 4):
                em("vector", lambda en, dst=dst, j=j, xs=xs, kk=kk: en.scalar_tensor_tensor(
                    xs, dst[:, kk:kk + 2048], cwb[:, 4 * j + kk:4 * j + kk + 1], xs,
                    ALU.mult, ALU.add), waits=[], inc=(sV if kk == 3 else None))
            raw_free[i % 2] = now(sV)
            em("scalar", lambda en, xs=xs: en.activation(xs, xs, AF.Silu),
               waits=[(sV, now(sV))], inc=sC)
            c_silu = now(sC)
            if role == "bc":
                em("vector", lambda en, xs=xs: en.tensor_copy(bcs[:, :], xs),
                   waits=[(sC, c_silu)], inc=sV)
                mm["bcs_done"] = now(sV)
                xs_free[xb] = [(sV, now(sV))]
            else:
                em("sync", lambda en, xs=xs, j=j: en.dma_start(out=d_xtm[j], in_=xs),
                   waits=[(sC, c_silu)], inc=wrS, amt=16)
                for c in range(NC):
                    em("scalar", lambda en, xs=xs, j=j, c=c: en.dma_start(
                        out=d_xcm[c][:, j, :], in_=xs[:, c * 256:(c + 1) * 256]),
                        waits=[], inc=wrA, amt=16)
                xs_free[xb] = [dw(wrS), dw(wrA)]
            continue

        zb = i % 4
        zi = e
        dstz = ztst(zb)

        def evac_z(g, pbuf, dstz=dstz):
            tmm = now(sT)
            em("vector", lambda en, g=g, pbuf=pbuf, dstz=dstz: en.tensor_copy(
                dstz[:, g * 512:(g + 1) * 512], pbuf[:, :]),
                waits=[(sT, tmm)], inc=sV)
        inproj(i, list(z_free[zb]), evac_z)
        em("scalar", lambda en, dstz=dstz: en.activation(dstz, dstz, AF.Silu),
           waits=[(sV, now(sV))], inc=sC)
        c_zs = now(sC)
        for c in range(NC):
            em("scalar", lambda en, dstz=dstz, zi=zi, c=c: en.dma_start(
                out=d_zcm[c][:, zi, :], in_=dstz[:, c * 256:(c + 1) * 256]),
                waits=[], inc=wrA, amt=16)
        z_free[zb] = [dw(wrA)]
        if zi % 4 == 3:
            emit_mini_chunk(minis)
            minis += 1
    assert minis == NC

    ckq = {}
    pvq = {}
    ebq = {}
    ecq = {}
    mdone = {}

    def load_chunk(c, extra=()):
        em("scalar", lambda e, c=c: e.dma_start(out=xck(c), in_=d_xcm[c][:, :, :]),
           waits=list(extra) +
                 ([(sV, mm[("evg", c - 2, 7)]), (sT, mm[("ymmg", c - 2, 7)])] if c >= 2 else []) +
                 [dw(wrA)],
           inc=dCK, amt=16)
        em("scalar", lambda e, c=c: e.dma_start(out=zck(c), in_=d_zcm[c][:, :, :]),
           waits=[], inc=dCK, amt=16)
        ckq[c] = now(dCK)
        em("gpsimd", lambda e, c=c: e.dma_start(out=prevb(c), in_=d_prevd[c]),
           waits=[dw(wrG)] + ([(sT, mm[("ymmg", c - 2, 7)])] if c >= 2 else []),
           inc=dG, amt=16)
        pvq[c] = now(dG)

    def load_EBc(c, qq, wait_marks):
        em("sync", lambda e, c=c, qq=qq: e.dma_start(
            out=Ebc(c, qq),
            in_=d_cumd[qq * 8:(qq + 1) * 8, c * 256:(c + 1) * 256].partition_broadcast(128)),
            waits=wait_marks + [dw(wrS)], inc=dEB, amt=16)
        ebq[(c, qq)] = now(dEB)

    def load_EC(c, qq, wait_marks):
        em("sync", lambda e, c=c, qq=qq: e.dma_start(
            out=ecumb(qq),
            in_=d_ecumd[qq * 8:(qq + 1) * 8, c * 256:(c + 1) * 256].partition_broadcast(64)),
            waits=list(wait_marks), inc=dEC, amt=16)
        em("sync", lambda e, c=c, qq=qq: e.dma_start(
            out=ecumbG(qq),
            in_=d_ecumgd[qq * 8:(qq + 1) * 8, c * 256:(c + 1) * 256].partition_broadcast(64)),
            waits=[], inc=dEC, amt=16)
        ecq[(c, qq)] = now(dEC)

    xtq = {}

    def load_xT(c, extra=()):
        for zt in range(2):
            em("scalar", lambda e, c=c, zt=zt: e.dma_start(
                out=xdtT_b(zt), in_=x2d[0:4096, (2 * c + zt) * 128:(2 * c + zt) * 128 + 128],
                transpose=True),
                waits=((list(extra) +
                        ([(sT, mm[("ymmg", c - 1, 7)])] if c > 0 else []))
                       if zt == 0 else []),
                inc=dXT, amt=16)
        xtq[c] = now(dXT)
        if c == 0:
            em("vector", lambda e: e.tensor_copy(xdtT_b(0)[0:3, :], fixT[0:3, :]),
               waits=[(dXT, xtq[0]), (sV, mm["fixT_done"])], inc=sV)
            mm["fix0_done"] = now(sV)

    b_gate = [(sT, now(sT)), (sV, now(sV)), (sC, now(sC)), dw(wrG)]
    em("gpsimd", lambda e: e.dma_start(out=BIG1[64:128, 12288:16384], in_=d_cc2o[0:64, :]),
       waits=[(sT, now(sT)), (sCC, mm["cc2_done"])], inc=dG, amt=16)
    em("gpsimd", lambda e: e.dma_start(out=BIG1[64:128, 16384:20480], in_=d_cc2o[0:64, :]),
       waits=[], inc=dG, amt=16)
    mm["crecv_q2"] = now(dG)
    DIall = BIG1[:, 20480:24576]
    for h in range(NH):
        em("vector", lambda e, h=h: e.tensor_scalar_mul(
            DIall[:, h * 128:(h + 1) * 128], identb[:, :], Dbc[:, h:h + 1]),
            waits=[], inc=(sV if h == NH - 1 else None))
    mm["di_done"] = now(sV)
    load_xT(0, extra=list(b_gate))
    load_chunk(0, extra=[(sT, now(sT))])
    em("vector", lambda e: e.tensor_copy(
        xck(0).rearrange("p (j t) -> p j t", t=256)[:, :, 0:3],
        fixb[:, 0:96].rearrange("p (j a) -> p j a", a=3)),
        waits=[(dCK, ckq[0]), (sC, mm["fix_done"])], inc=sV)
    load_chunk(1)
    b_gate2 = [(sT, now(sT)), (sV, now(sV)), (sC, now(sC)), dw(wrG)]
    load_EBc(0, 0, list(b_gate2))
    load_EBc(0, 1, list(b_gate2))
    load_EBc(0, 2, list(b_gate2))
    load_EC(0, 0, list(b_gate2))
    load_EC(0, 1, list(b_gate2))

    def pre_gt(c):
        for zt in range(2):
            em("tensor", lambda e, c=c, zt=zt: e.matmul(
                PU[:, zt * 256:(zt + 1) * 256],
                bcs[0:64, c * 256 + zt * 128: c * 256 + zt * 128 + 128],
                bcsC[0:64, c * 256:(c + 1) * 256], start=True, stop=True),
                waits=(([(sV, mm["bcs_fixed"]), (dG, mm["bcsC_q"]), (sV, mm["pu_free"])] +
                        list(b_gate)
                        if c == 0 else [(sV, mm[("gt_ev", c - 1)])]) if zt == 0 else []),
                inc=sT)
        t_gt = now(sT)
        for zt in range(2):
            em("vector", lambda e, c=c, zt=zt: e.tensor_tensor(
                GtsS(c)[:, zt * 256:(zt + 1) * 256], PU[:, zt * 256:(zt + 1) * 256],
                trilm[:, zt * 256:(zt + 1) * 256], ALU.mult),
                waits=([(sT, t_gt)] if zt == 0 else []),
                inc=(sV if zt == 1 else None))
        mm[("gt_ev", c)] = now(sV)

    def stage1(G):
        c, g4 = divmod(G, 8)
        qq = g4 // 2
        for i in range(4):
            h = 4 * g4 + i
            for zt in range(2):
                blk = (c * 2 + zt) * 32
                em("vector", lambda e, G=G, h=h, i=i, zt=zt, blk=blk, qq=qq, c=c: e.tensor_scalar(
                    MhH(G, i)[:, zt * 256:(zt + 1) * 256],
                    Ebc(c, qq)[:, ((h % 8) * 256):((h % 8) * 256 + 256)],
                    ncumT[:, blk + h:blk + h + 1], 0.0, ALU.add, ALU.min),
                    waits=(([(dEB, ebq[(c, qq)])] +
                            ([(sT, mm[("ymmG", G - 3)])] if G >= 3 else []) +
                            (list(b_gate2) if G <= 2 else []))
                           if (i == 0 and zt == 0) else []),
                    inc=(sV if (i == 3 and zt == 1) else None))
        mm[("v_e", G)] = now(sV)
        em("scalar", lambda e, G=G: e.activation(MhS(G)[:, :], MhS(G)[:, :], AF.Exp),
           waits=[(sV, mm[("v_e", G)])], inc=sC)
        mm[("c_exp", G)] = now(sC)

    def stage2(G):
        c, g4 = divmod(G, 8)
        qq = g4 // 2
        gp = g4 % 2
        bank = [(PB0, PB1), (PG, PQ)][gp]
        em("vector", lambda e, c=c, gp=gp, qq=qq: e.tensor_tensor(
            Cxp[:, gp * 1024:gp * 1024 + 1024].rearrange("p (i t) -> p i t", t=256),
            bcsC[:, c * 256:(c + 1) * 256].unsqueeze(1).broadcast_to([128, 4, 256]),
            ecumbF(qq)[:, gp * 1024:gp * 1024 + 1024].rearrange("p (i t) -> p i t", t=256),
            ALU.mult),
            waits=[(dEC, ecq[(c, qq)]), (sV, mm["di_done"])] +
                  ([(sT, mm[("ymmG", G - 2)])] if G >= 2 else []),
            inc=sV)
        mm[("v_aux", G)] = now(sV)
        for i in range(4):
            h = 4 * g4 + i
            for zt in range(2):
                blk = (c * 2 + zt) * 32
                em("vector", lambda e, G=G, i=i, h=h, zt=zt, blk=blk, c=c: e.scalar_tensor_tensor(
                    MhH(G, i)[:, zt * 256:(zt + 1) * 256],
                    MhH(G, i)[:, zt * 256:(zt + 1) * 256],
                    dtTb[:, blk + h:blk + h + 1],
                    GtsS(c)[:, zt * 256:(zt + 1) * 256],
                    ALU.mult, ALU.mult),
                    waits=([(sC, mm[("c_exp", G)])] if (i == 0 and zt == 0) else []),
                    inc=(sV if (i == 3 and zt == 1) else None))
        mm[("v_m", G)] = now(sV)
        for i in range(4):
            h = 4 * g4 + i
            pY = bank[i // 2][:, (i % 2) * 256:(i % 2) * 256 + 256]
            em("tensor", lambda e, G=G, h=h, i=i, pY=pY, c=c: e.matmul(
                pY, xdtT_b(0)[:, h * 128:(h + 1) * 128], MhH(G, i)[:, 0:256],
                start=True, stop=False),
                waits=(([(sV, mm[("v_m", G)]), (sV, mm[("v_aux", G)]), (dCK, ckq[c]),
                         (dXT, xtq[c])] +
                        ([(sV, mm[("evG", G - 2)])] if G >= 2 else []) +
                        ([(sV, mm["fix0_done"])] if c == 0 and g4 == 0 and i == 0 else []) +
                        ([(dG, mm["crecv_q2"]), (dG, pvq[c])] if g4 == 0 else []))
                       if i == 0 else []),
                inc=None)
            em("tensor", lambda e, G=G, h=h, i=i, pY=pY: e.matmul(
                pY, xdtT_b(1)[:, h * 128:(h + 1) * 128], MhH(G, i)[:, 256:512],
                start=False, stop=False), waits=[], inc=None)
            em("tensor", lambda e, c=c, h=h, pY=pY: e.matmul(
                pY, DIall[:, h * 128:(h + 1) * 128],
                xck(c)[:, h * 256:(h + 1) * 256], start=False, stop=False),
                waits=[], inc=None)
            em("tensor", lambda e, c=c, h=h, i=i, gp=gp, pY=pY: e.matmul(
                pY, prevbF(c)[:, h * 128:(h + 1) * 128],
                Cxp[:, gp * 1024 + i * 256:gp * 1024 + (i + 1) * 256],
                start=False, stop=True),
                waits=[],
                inc=(sT if i == 3 else None))
        mm[("ymmG", G)] = now(sT)
        mm[("ymmg", c, g4)] = now(sT)

    def stage3(G):
        c, g4 = divmod(G, 8)
        bank = [(PB0, PB1), (PG, PQ)][g4 % 2]
        for pr in range(2):
            h0 = 4 * g4 + 2 * pr
            em("vector", lambda e, c=c, h0=h0, bk=bank[pr]: e.tensor_tensor(
                BIG1[:, 2048 + (h0 % 4) * 256: 2048 + (h0 % 4) * 256 + 512],
                bk[:, 0:512], zck(c)[:, h0 * 256:(h0 + 2) * 256], ALU.mult),
                waits=([(sT, mm[("ymmG", G)])] if pr == 0 else []), inc=None)
        for pr in range(2):
            h0 = 4 * g4 + 2 * pr
            em("vector", lambda e, h0=h0: e.tensor_tensor(
                BIG1[:, 3072 + (h0 % 4) * 256: 3072 + (h0 % 4) * 256 + 512],
                BIG1[:, 2048 + (h0 % 4) * 256: 2048 + (h0 % 4) * 256 + 512],
                BIG1[:, 2048 + (h0 % 4) * 256: 2048 + (h0 % 4) * 256 + 512], ALU.mult),
               waits=[], inc=None)
        for i in range(4):
            h = 4 * g4 + i
            em("vector", lambda e, h=h: e.tensor_scalar_mul(ynt(h), ygt(h), nwb[:, h:h + 1]),
               waits=([(wrG, mm.get(("ynwG", G - 2), 0))] if i == 0 else []),
               inc=(sV if i == 3 else None))
        mm[("evG", G)] = now(sV)
        mm[("evg", c, g4)] = now(sV)
        for i2 in range(4):
            h2 = 4 * g4 + i2
            em("tensor", lambda e, h2=h2: e.matmul(
                PS[0:1, 0:256], onesc[:, :], sqt(h2),
                start=(h2 == 0), stop=(h2 == NH - 1)),
                waits=([(sV, mm[("evG", G)])] if i2 == 0 else []) +
                      ([(sC, mm[("rinv", c - 1)])] if (h2 == 0 and c > 0) else []),
                inc=(sT if h2 == NH - 1 else None))
        if g4 == 7:
            mm[("t_rms", c)] = now(sT)
        em("gpsimd", lambda e, c=c, g4=g4: e.dma_start(
            out=d_yns[c][:, 4 * g4:4 * g4 + 4, :], in_=yntg(g4)),
            waits=[(sV, mm[("evG", G)]), (wrG, now(wrG))], inc=wrG, amt=16)
        mm[("ynwG", G)] = now(wrG)

    def rinv_ep(c):
        em("scalar", lambda e, c=c: e.activation(
            rinvr[0:1, 0:256], PS[0:1, 0:256], AF.Ln,
            bias=epsc[0:1, 1:2], scale=1.0 / 4096.0),
            waits=[(sT, mm[("t_rms", c)])] + ([dw(wrS)] if c > 0 else []), inc=None)
        em("scalar", lambda e, c=c: e.activation(
            rinvr[0:1, 0:256], rinvr[0:1, 0:256],
            AF.Exp, scale=-0.5), waits=[], inc=sC)
        mm[("rinv", c)] = now(sC)
        em("sync", lambda e, c=c: e.dma_start(
            out=d_rows[2:3, c * 256:(c + 1) * 256], in_=rinvr[0:1, 0:256]),
            waits=[(sC, mm[("rinv", c)])], inc=wrS, amt=16)

    for G in range(66):
        c, g4 = divmod(G, 8)
        if G < 64:
            if g4 == 0:
                pre_gt(c)
            stage1(G)
        if 1 <= G <= 64:
            Gp = G - 1
            stage2(Gp)
            cp, gq = divmod(Gp, 8)
            if gq == 7 and cp + 1 < NC:
                load_xT(cp + 1)
            if gq % 2 == 1:
                nq = gq // 2 + 3
                if nq < 4:
                    load_EBc(cp, nq, [(sV, mm[("v_e", Gp)])])
                elif cp + 1 < NC:
                    load_EBc(cp + 1, nq - 4, [(sV, mm[("v_e", Gp)])])
                nq2 = gq // 2 + 2
                if nq2 < 4:
                    load_EC(cp, nq2, [(sV, mm[("v_aux", Gp)])])
                elif cp + 1 < NC:
                    load_EC(cp + 1, nq2 - 4, [(sV, mm[("v_aux", Gp)])])
        if G >= 2:
            Gp = G - 2
            stage3(Gp)
            cp, gq = divmod(Gp, 8)
            if gq == 7:
                rinv_ep(cp)
                if cp + 2 < NC:
                    load_chunk(cp + 2)

    em("gpsimd", lambda e: e.dma_start(out=rinvb, in_=d_rows[2:3, :].partition_broadcast(128)),
       waits=[dw(wrS)], inc=dG, amt=16)
    q_rinv = now(dG)

    woq = {}

    def load_wout(H, dd):
        par = dd % 2
        em("gpsimd", lambda e, dd=dd, par=par: e.dma_start(
            out=wwb[:, par * 4096:(par + 1) * 4096], in_=d_wout[dd]),
            waits=([(sT, mm[("omm", H, dd - 2)])] if dd >= 2 else
                   ([(sT, mm[("omm", H - 1, 14 + dd)])] if H > 0 else [])),
            inc=dWO, amt=16)
        woq[(H, dd)] = now(dWO)

    cev = {}
    for H in range(2):
        for j in range(32):
            for cc in range(4):
                em("sync", lambda e, H=H, j=j, cc=cc: e.dma_start(
                    out=ynb(j)[:, cc * 256:(cc + 1) * 256], in_=d_yns[4 * H + cc][:, j, :]),
                    waits=([dw(wrA), (sV, mm[("evg", NC - 1, 7)])] if (j == 0 and cc == 0 and H == 0)
                           else ([(sT, mm[("omm", 0, 15)])] if (j == 0 and cc == 0 and H == 1) else [])),
                    inc=dYN, amt=16)
        q_yn = now(dYN)
        if H == 0:
            load_wout(0, 0)
            load_wout(0, 1)
        for dd in range(16):
            par = dd % 2
            em("sync", lambda e, dd=dd, H=H, par=par: e.dma_start(
                out=ureb(par), in_=d_uT[dd * 128:(dd + 1) * 128, H * 1024:(H + 1) * 1024]),
                waits=([(sV, cev[(H, dd - 2)])] if dd >= 2 else
                       ([(sV, cev[(H - 1, 14 + dd)])] if H > 0 else [])),
                inc=dUR, amt=16)
            q_ur = now(dUR)
            for g in range(2):
                pbuf = [PB0, PB1, PG, PQ][(dd * 2 + g) % 4]
                for k in range(32):
                    em("tensor", lambda e, dd=dd, H=H, g=g, k=k, par=par, pbuf=pbuf: e.matmul(
                        pbuf[:, :], wwb[:, par * 4096 + k * 128: par * 4096 + (k + 1) * 128],
                        ynb(k)[:, g * 512:(g + 1) * 512],
                        start=(k == 0), stop=(k == 31)),
                        waits=([(dWO, woq[(H, dd)]), (dYN, q_yn)] +
                               ([(sV, cev[(H, dd - 2)])] if dd >= 2 else
                                ([(sV, cev[(H - 1, 14 + dd)])] if H > 0 else []))
                               if k == 0 else []),
                        inc=(sT if k == 31 else None))
                t_omm = now(sT)
                if g == 1:
                    mm[("omm", H, dd)] = t_omm
                em("vector", lambda e, H=H, g=g, pbuf=pbuf: e.tensor_tensor(
                    x1t(g), pbuf[:, :], rinvb[:, H * 1024 + g * 512: H * 1024 + (g + 1) * 512],
                    ALU.mult), waits=[(sT, t_omm), (dG, q_rinv), (wrA, now(wrA))], inc=None)
                em("vector", lambda e, g=g, par=par: e.tensor_tensor(
                    x1t(g), x1t(g), ureb(par)[:, g * 512:(g + 1) * 512], ALU.add),
                    waits=[(dUR, q_ur)], inc=None)
                em("vector", lambda e, dd=dd, H=H, g=g: e.tensor_copy(
                    x1b(dd)[:, H * 1024 + g * 512: H * 1024 + (g + 1) * 512], x1t(g)),
                    waits=[], inc=sV)
                em("scalar", lambda e, dd=dd, H=H, g=g: e.dma_start(
                    out=d_x1[dd][:, H * 1024 + g * 512: H * 1024 + (g + 1) * 512], in_=x1t(g)),
                    waits=[(sV, now(sV))], inc=wrA, amt=16)
            cev[(H, dd)] = now(sV)
            if dd + 2 < 16:
                load_wout(H, dd + 2)
            elif H == 0 and dd == 14:
                load_wout(1, 0)
            elif H == 0 and dd == 15:
                load_wout(1, 1)
    v_x1done = now(sV)
    t_c_end = now(sT)

    emit_ln_stats(x1b, sqd, [(sV, v_x1done)])
    em("gpsimd", lambda e: e.dma_start(out=QbD, in_=d_rows[0:1, :].partition_broadcast(128)),
       waits=[dw(wrS)], inc=dG, amt=16)
    em("gpsimd", lambda e: e.dma_start(out=PbD, in_=d_rows[1:2, :].partition_broadcast(128)),
       waits=[], inc=dG, amt=16)
    q_qp2 = now(dG)
    for k in range(NK):
        em("vector", lambda e, k=k: e.tensor_tensor(x1b(k), x1b(k), QbD, ALU.mult),
           waits=([(dG, q_qp2)] if k == 0 else []), inc=None)
        em("vector", lambda e, k=k: e.tensor_tensor(x1b(k), x1b(k), PbD, ALU.subtract),
           waits=[], inc=(sV if k == NK - 1 else None))
    v_xh = now(sV)

    for o in range(16):
        em("gpsimd", lambda e, o=o: e.dma_start(out=swb(o), in_=d_wffn[o]),
           waits=([(sT, t_c_end)] if o == 0 else []), inc=dWO, amt=16)
        q_w = now(dWO)
        em("scalar", lambda e, o=o: e.activation(abst, swb(o), AF.Abs),
           waits=[(dWO, q_w)] + ([(sT, now(sT))] if o > 0 else [(sV, v_x1done)]), inc=None)
        em("scalar", lambda e, o=o: e.sign(swb(o), swb(o)), waits=[], inc=sC)
        c_pr = now(sC)
        for k in range(NK):
            em("tensor", lambda e, o=o, k=k: e.matmul(
                PV[:, 0:1], abst[:, k * 128:(k + 1) * 128], onesc[:, :],
                start=(k == 0), stop=(k == NK - 1)),
                waits=([(sC, c_pr), (sV, now(sV))] if k == 0 else []),
                inc=(sT if k == NK - 1 else None))
        em("vector", lambda e, o=o: e.tensor_scalar_mul(alf[:, o:o + 1], PV[:, 0:1], 1.0 / 2048.0),
           waits=[(sT, now(sT))], inc=sV)
    v_alf = now(sV)

    x1q = {}

    def load_x1r(o):
        par = o % 2
        em("gpsimd", lambda e, o=o, par=par: e.dma_start(out=x1rb(par), in_=d_x1[o]),
           waits=([dw(wrA), (sV, v_xh)] if o < 2 else [(sV, mm[("fev", o - 2, NG - 1)])]),
           inc=dG, amt=16)
        x1q[o] = now(dG)

    load_x1r(0)
    load_x1r(1)
    for o in range(16):
        par = o % 2
        for g in range(NG):
            pbuf = [PB0, PB1, PG, PQ][g]
            for k in range(NK):
                em("tensor", lambda e, o=o, g=g, k=k, pbuf=pbuf: e.matmul(
                    pbuf[:, :], swb(o)[:, k * 128:(k + 1) * 128],
                    x1b(k)[:, g * 512:(g + 1) * 512],
                    start=(k == 0), stop=(k == NK - 1)),
                    waits=(([(sV, v_xh), (sV, v_alf)] +
                            ([(sV, mm[("fev", o - 1, g)])] if o > 0 else []))
                           if k == 0 else []),
                    inc=(sT if k == NK - 1 else None))
            t_f = now(sT)
            em("vector", lambda e, o=o, g=g, pbuf=pbuf: e.tensor_scalar(
                d_ot(g), pbuf[:, :], alf[:, o:o + 1], fbb[:, o:o + 1], ALU.mult, ALU.add),
                waits=[(sT, t_f), (outw, now(outw))] +
                      ([(sV, mm[("fev", o, g - 2)])] if g >= 2 else
                       ([(sV, mm[("fev", o - 1, g + 2)])] if o > 0 else [])),
                inc=None)
            em("vector", lambda e, o=o, g=g, par=par: e.tensor_tensor(
                d_ot(g), d_ot(g), x1rb(par)[:, g * 512:(g + 1) * 512], ALU.add),
                waits=[(dG, x1q[o])], inc=sV)
            mm[("fev", o, g)] = now(sV)
            em("scalar", lambda e, o=o, g=g: e.dma_start(
                out=d_out[o * 128:(o + 1) * 128, g * 512:(g + 1) * 512], in_=d_ot(g)),
                waits=[(sV, mm[("fev", o, g)])], inc=outw, amt=16)
        if o + 2 < 16:
            load_x1r(o + 2)

    final_outw = now(outw)
    assert final_outw == 16 * 64

    def replay(name, eng):
        for (e, fn, waits, inc, amt) in steps:
            if e != name:
                continue
            for (s, v) in waits:
                eng.wait_ge(s, v)
            inst = fn(eng)
            if inc is not None:
                inst.then_inc(inc, amt)

    with nc.Block() as block:
        @block.sync
        def _(e):
            replay("sync", e)
            e.wait_ge(outw, final_outw)

        @block.tensor
        def _(e):
            replay("tensor", e)

        @block.vector
        def _(e):
            replay("vector", e)

        @block.scalar
        def _(e):
            replay("scalar", e)

        @block.gpsimd
        def _(e):
            replay("gpsimd", e)

    return nc


def kernel(**inputs):
    u = np.asarray(inputs["L0_attn_u"], dtype=np.float32)
    Win = np.asarray(inputs["L0_attn_W_in"], dtype=np.float32)
    convw = np.asarray(inputs["L0_attn_conv_w"], dtype=np.float32)
    convb = np.asarray(inputs["L0_attn_conv_b"], dtype=np.float32)
    dtb = np.asarray(inputs["L0_attn_dt_bias"], dtype=np.float32)
    alog = np.asarray(inputs["L0_attn_A_log"], dtype=np.float32)
    dv = np.asarray(inputs["L0_attn_D"], dtype=np.float32)
    nw = np.asarray(inputs["L0_attn_norm_w"], dtype=np.float32)
    wout = np.asarray(inputs["L0_attn_W_out"], dtype=np.float32)
    wffn = np.asarray(inputs["L0_ffn_weight"], dtype=np.float32)
    fbias = np.asarray(inputs["L0_ffn_bias"], dtype=np.float32)

    winp = np.zeros((NE * 128, D), dtype=np.float32)
    winp[:8352] = Win
    w_in_st = np.ascontiguousarray(winp.reshape(NE, 128, NK, 128).transpose(0, 3, 2, 1))
    w_out_st = np.ascontiguousarray(wout.reshape(16, 128, 32, 128).transpose(0, 3, 2, 1))
    w_ffn_st = np.ascontiguousarray(wffn.reshape(16, 128, 16, 128).transpose(0, 3, 2, 1))
    cw_st = np.ascontiguousarray(convw.reshape(33, 128, 4).transpose(1, 0, 2))
    cb_st = np.ascontiguousarray(convb.reshape(33, 128).T)
    nw_st = np.ascontiguousarray(nw.reshape(32, 128).T)
    fb_st = np.ascontiguousarray(fbias.reshape(16, 128).T)
    tril = np.zeros((2, 128, 256), dtype=np.float32)
    for zt in range(2):
        for zp in range(128):
            tril[zt, zp, zt * 128 + zp:] = 1.0
    base = {
        "w_in": w_in_st, "w_out": w_out_st, "w_ffn": w_ffn_st,
        "conv_w": cw_st, "conv_b": cb_st,
        "dt_bias": dtb.reshape(32, 1), "A_log": alog.reshape(32, 1),
        "Dvec": dv.reshape(1, 32), "norm_w": nw_st, "ffn_bias": fb_st,
        "ones_r": np.ones((1, 128), dtype=np.float32),
        "ones_c": np.ones((128, 1), dtype=np.float32),
        "ident": np.eye(128, dtype=np.float32), "trilm_d": tril,
    }
    in_maps = []
    for core in range(8):
        b, hf = core // 2, core % 2
        m = dict(base)
        m["uT"] = np.ascontiguousarray(u[b, hf * T:(hf + 1) * T, :].T)
        m["flagv"] = np.array([[float(hf), EPS, 1.0, 0.0]], dtype=np.float32)
        in_maps.append(m)

    nc = _build()
    res = run_bass_kernel_spmd(nc, in_maps, core_ids=list(range(8)))
    out = np.empty((4, 4096, D), dtype=np.float32)
    for core in range(8):
        b, hf = core // 2, core % 2
        out[b, hf * T:(hf + 1) * T, :] = res.results[core]["out_dT"].T
    return out


# version 13 (final)
# speedup vs baseline: 1.0138x; 1.0138x over previous
import numpy as np
import concourse.bass as bass
import concourse.mybir as mybir
from concourse.bass_utils import run_bass_kernel_spmd

F32 = mybir.dt.float32
BF16 = mybir.dt.bfloat16
AF = mybir.ActivationFunctionType
ALU = mybir.AluOpType

D = 2048
NK = 16
NE = 66
T = 2048
NG = 4
NC = 8
CH = 256
NH = 32
EPS = 1e-5

ETILE_ORDER = [65, 64] + list(range(32, 64)) + list(range(0, 32))


def _build():
    nc = bass.Bass("TRN2", target_bir_lowering=False)

    d_uT = nc.dram_tensor("uT", [D, T], F32, kind="ExternalInput")
    d_win = nc.dram_tensor("w_in", [NE, 128, NK, 128], F32, kind="ExternalInput")
    d_wout = nc.dram_tensor("w_out", [16, 128, 32, 128], F32, kind="ExternalInput")
    d_wffn = nc.dram_tensor("w_ffn", [16, 128, 16, 128], F32, kind="ExternalInput")
    d_cw = nc.dram_tensor("conv_w", [128, 33, 4], F32, kind="ExternalInput")
    d_cb = nc.dram_tensor("conv_b", [128, 33], F32, kind="ExternalInput")
    d_dtb = nc.dram_tensor("dt_bias", [32, 1], F32, kind="ExternalInput")
    d_alog = nc.dram_tensor("A_log", [32, 1], F32, kind="ExternalInput")
    d_dv = nc.dram_tensor("Dvec", [1, 32], F32, kind="ExternalInput")
    d_nw = nc.dram_tensor("norm_w", [128, 32], F32, kind="ExternalInput")
    d_fb = nc.dram_tensor("ffn_bias", [128, 16], F32, kind="ExternalInput")
    d_ones_r = nc.dram_tensor("ones_r", [1, 128], F32, kind="ExternalInput")
    d_ones_c = nc.dram_tensor("ones_c", [128, 1], F32, kind="ExternalInput")
    d_ident = nc.dram_tensor("ident", [128, 128], F32, kind="ExternalInput")
    d_trilm = nc.dram_tensor("trilm_d", [2, 128, 256], F32, kind="ExternalInput")
    d_flag = nc.dram_tensor("flagv", [1, 4], F32, kind="ExternalInput")

    d_xtm = nc.dram_tensor("x_tm", [32, 128, T], BF16, kind="Internal")
    d_xcm = nc.dram_tensor("x_cm", [NC, 128, NH, CH], BF16, kind="Internal")
    d_zcm = nc.dram_tensor("z_cm", [NC, 128, NH, CH], BF16, kind="Internal")
    d_yns = nc.dram_tensor("yn_s", [NC, 128, NH, CH], BF16, kind="Internal")
    d_x1 = nc.dram_tensor("x1_s", [16, 128, T], F32, kind="Internal")
    d_cumd = nc.dram_tensor("cum_d", [32, T], F32, kind="Internal")
    d_ecumd = nc.dram_tensor("ecum_d", [32, T], BF16, kind="Internal")
    d_ecumgd = nc.dram_tensor("ecumg_d", [32, T], BF16, kind="Internal")
    d_prevd = nc.dram_tensor("prev_d", [NC, 64, NH * 128], BF16, kind="Internal")
    d_rows = nc.dram_tensor("rows_d", [4, T], F32, kind="Internal")
    d_csp = nc.dram_tensor("c_split", [64, T], BF16, kind="Internal")
    d_cdm = nc.dram_tensor("cd_m", [32, 64], F32, kind="Internal")
    d_cc1 = nc.dram_tensor("cc1_in", [128, 99], F32, kind="Internal")
    d_cc1o = nc.dram_tensor("cc1_out", [256, 99], F32, kind="Internal")
    d_cc2 = nc.dram_tensor("cc2_in", [64, NH * 128], BF16, kind="Internal")
    d_cc2o = nc.dram_tensor("cc2_out", [128, NH * 128], BF16, kind="Internal")

    d_out = nc.dram_tensor("out_dT", [D, T], F32, kind="ExternalOutput")

    ctx = []

    def sb(name, shape, dt):
        t = nc.sbuf_tensor(name, shape, dt)
        ctx.append(t)
        return t.__enter__()

    def psum(name, shape, dt):
        t = nc.psum_tensor(name, shape, dt)
        ctx.append(t)
        return t.__enter__()

    def mksem(name):
        t = nc.semaphore(name)
        ctx.append(t)
        return t.__enter__()

    BIG1 = sb("BIG1", [128, 32768], BF16)
    BIG2 = sb("BIG2", [128, 32768], BF16)
    BIG3 = sb("BIG3", [128, 16384], BF16)
    onesr = sb("onesr", [1, 128], BF16)
    onesc = sb("onesc", [128, 1], BF16)
    identb = sb("identb", [128, 128], BF16)
    trilm = sb("trilm", [128, 512], BF16)
    cwb = sb("cwb", [128, 132], F32)
    cbb = sb("cbb", [128, 33], F32)
    dtbias = sb("dtbias", [32, 1], F32)
    Aneg = sb("Aneg", [32, 1], F32)
    Dbc = sb("Dbc", [128, 32], F32)
    nwb = sb("nwb", [128, 32], F32)
    fbb = sb("fbb", [128, 16], F32)
    alf = sb("alf", [128, 16], F32)
    flagc = sb("flagc", [128, 1], F32)
    epsc = sb("epsc", [1, 4], F32)
    oneb32 = sb("oneb32", [32, 1], F32)
    tinyb = sb("tinyb", [1, 64], BF16)
    ncumT = sb("ncumT", [128, 512], F32)
    dtTb = sb("dtTb", [128, 512], BF16)
    decT = sb("decT", [128, 512], F32)
    cdrows = sb("cdrows", [32, 64], F32)
    cdtr = sb("cdtr", [32, 64], F32)
    cdrow0 = sb("cdrow0", [1, 512], BF16)
    cdsb = sb("cdsb", [64, 32], F32)

    rinvr = sb("rinvr", [1, 256], F32)
    halos = sb("halos", [128, 99], BF16)
    halor = sb("halor", [128, 99], BF16)
    keep3 = sb("keep3", [128, 99], BF16)
    fixw = sb("fixw", [128, 99], F32)
    fixw2 = sb("fixw2", [128, 99], F32)
    fixb = sb("fixb", [128, 99], BF16)
    fixraw = sb("fixraw", [128, 198], F32)
    bcs = sb("bcs", [128, 2048], BF16)
    bcsC = sb("bcsC", [128, 2048], BF16)
    BCt = sb("BCt", [128, 1024], BF16)
    Bdecb = sb("Bdecb", [128, 1024], BF16)
    DIt = sb("DIt", [128, 512], BF16)
    Cxp = sb("Cxp", [128, 2048], BF16)
    wwb = sb("wwb", [128, 8192], BF16)

    PB0 = psum("PB0", [128, 512], F32)
    PB1 = psum("PB1", [128, 512], F32)
    PG = psum("PG", [128, 512], F32)
    PQ = psum("PQ", [128, 512], F32)
    PS = psum("PS", [128, 512], F32)
    PV = psum("PV", [128, 512], F32)
    PU = psum("PU", [128, 512], F32)
    PT = psum("PT", [128, 1024], BF16)

    sT = mksem("sT"); sV = mksem("sV"); sC = mksem("sC"); sCC = mksem("sCC")
    dW0 = mksem("dW0"); dW1 = mksem("dW1")
    dU = mksem("dU"); dG = mksem("dG")
    dXT = mksem("dXT"); dEB = mksem("dEB"); dEC = mksem("dEC")
    dCK = mksem("dCK"); dYN = mksem("dYN"); dWO = mksem("dWO"); dUR = mksem("dUR")
    wrS = mksem("wrS"); wrA = mksem("wrA"); wrG = mksem("wrG"); outw = mksem("outw")

    def f32v(t, lo, n, p0=0, p1=128):
        return t[p0:p1, lo:lo + 2 * n].bitcast(F32)

    statr3 = f32v(BIG3, 0, 512, 0, 1)
    statr = f32v(BIG3, 1024, 512, 0, 1)
    statr2 = f32v(BIG3, 2048, 512, 0, 1)

    ub = lambda k: BIG1[:, k * 2048:(k + 1) * 2048]
    x1b = ub
    ynt = lambda h: BIG1[:, (h % 8) * 256:(h % 8) * 256 + 256]
    yntg = lambda g4: BIG1[:, (g4 % 2) * 1024:(g4 % 2) * 1024 + 1024]
    ygt = lambda h: BIG1[:, 2048 + (h % 4) * 256: 2048 + (h % 4) * 256 + 256]
    sqt = lambda h: BIG1[:, 3072 + (h % 4) * 256: 3072 + (h % 4) * 256 + 256]
    MhS = lambda G: [BIG1[:, 4096:6144], BIG1[:, 6144:8192], BIG3[:, 13312:15360]][G % 3]
    MhH = lambda G, i: MhS(G)[:, i * 512:(i + 1) * 512]
    GtsS = lambda c: BIG3[:, 12288 + (c % 2) * 512: 12288 + (c % 2) * 512 + 512]
    ecumb = lambda qq: BIG1[0:64, 8192 + (qq % 2) * 2048: 8192 + (qq % 2) * 2048 + 2048]
    ecumbG = lambda qq: BIG1[64:128, 8192 + (qq % 2) * 2048: 8192 + (qq % 2) * 2048 + 2048]
    ecumbF = lambda qq: BIG1[0:128, 8192 + (qq % 2) * 2048: 8192 + (qq % 2) * 2048 + 2048]
    prevb = lambda c: BIG1[0:64, 12288 + (c % 2) * 4096: 12288 + (c % 2) * 4096 + 4096]
    prevbF = lambda c: BIG1[0:128, 12288 + (c % 2) * 4096: 12288 + (c % 2) * 4096 + 4096]
    crecv = BIG3[0:64, 8192:12288]
    tmp64 = BIG3[0:64, 12288:12800]
    carry = f32v(BIG3, 4096, 4096, 0, 64)
    carryb = BIG3[0:64, 12288:16384]
    xdtT_m = lambda zt: BIG2[:, zt * 4096:(zt + 1) * 4096]
    xdtT_b = lambda zt: BIG1[:, 24576 + zt * 4096: 24576 + (zt + 1) * 4096]

    sqb = lambda k: BIG2[:, k * 512:(k + 1) * 512]
    ztst = lambda p: BIG2[:, 20488 + (p % 4) * 2048: 20488 + (p % 4) * 2048 + 2048]
    xsst = lambda p: BIG2[:, 8192 + (p % 4) * 2048: 8192 + (p % 4) * 2048 + 2048]
    xraw = lambda p: BIG2[:, 16384 + (p % 2) * 2052: 16384 + (p % 2) * 2052 + 2052]
    xck = lambda c: BIG2[:, (c % 2) * 8192:(c % 2) * 8192 + 8192]
    zck = lambda c: BIG2[:, 16384 + (c % 2) * 8192: 16384 + (c % 2) * 8192 + 8192]
    ynb = lambda j: BIG2[:, j * 1024:(j + 1) * 1024]
    swb = lambda o: BIG2[:, o * 2048:(o + 1) * 2048]

    QbA = BIG3[:, 0:2048]
    PbA = BIG3[:, 2048:4096]
    Ebc = lambda c, qq: f32v(BIG3, ((4 * c + qq + 2) % 3) * 4096, 2048)
    rinvb = f32v(BIG3, 0, 2048)
    ureb = lambda p: f32v(BIG3, 4096 + (p % 2) * 2048, 1024)
    x1t = lambda g: f32v(BIG3, 12288 + (g % 2) * 1024, 512)
    QbD = BIG3[:, 0:2048]
    PbD = BIG3[:, 2048:4096]
    abst = BIG3[:, 4096:6144]
    d_ot = lambda g: f32v(BIG3, 6144 + (g % 2) * 1024, 512)
    sqd = lambda k: BIG3[:, 8192 + k * 512: 8192 + (k + 1) * 512]
    x1rb = lambda p: f32v(BIG3, 8192 + (p % 2) * 4096, 2048)

    rows_dt = f32v(BIG2, 0, 2048, 0, 32)
    rows_dA = f32v(BIG2, 4096, 2048, 0, 32)
    rows_cum = f32v(BIG3, 4096, 2048, 0, 32)
    rows_tmp = rows_dA
    fixT = BIG3[0:3, 0:4096]

    steps = []
    cnt = {}

    def em(eng, fn, waits=(), inc=None, amt=1):
        for (s, v) in waits:
            assert v <= cnt.get(id(s), 0), (eng, v, cnt.get(id(s), 0))
        steps.append((eng, fn, tuple(waits), inc, amt))
        if inc is not None:
            cnt[id(inc)] = cnt.get(id(inc), 0) + amt
        return cnt.get(id(inc)) if inc is not None else None

    def now(s):
        return cnt.get(id(s), 0)

    def dw(s):
        return (s, now(s))

    mm = {}

    def gl(out_ap, in_ap):
        em("gpsimd", lambda e, o=out_ap, i=in_ap: e.dma_start(out=o, in_=i), inc=dU, amt=16)

    gl(onesr[:, :], d_ones_r[:, :])
    gl(onesc[:, :], d_ones_c[:, :])
    gl(identb[:, :], d_ident[:, :])
    gl(trilm[:, 0:256], d_trilm[0])
    gl(trilm[:, 256:512], d_trilm[1])
    gl(cwb[:, 0:132], d_cw[:, :, :])
    gl(cbb[:, :], d_cb[:, :])
    gl(dtbias[:, :], d_dtb[:, :])
    gl(Aneg[:, :], d_alog[:, :])
    gl(nwb[:, :], d_nw[:, :])
    gl(fbb[:, :], d_fb[:, :])
    gl(tinyb[0:1, 0:1], d_flag[0:1, 0:1])
    gl(epsc[:, :], d_flag[:, :])
    gl(tinyb[0:1, 16:48], d_dv[:, :])
    for k in range(NK):
        gl(ub(k), d_uT[k * 128:(k + 1) * 128, :])

    em("vector", lambda e: e.memset(oneb32[:, :], 1.0), waits=[], inc=sV)
    em("scalar", lambda e: e.activation(Aneg[:, :], Aneg[:, :], AF.Exp),
       waits=[dw(dU)], inc=sC)
    em("vector", lambda e: e.tensor_scalar_mul(Aneg[:, :], Aneg[:, :], -1.0),
       waits=[(sC, now(sC))], inc=sV)
    em("tensor", lambda e: e.matmul(PU[:, 0:1], onesr[0:1, :], tinyb[0:1, 0:1],
                                    start=True, stop=True), waits=[dw(dU)], inc=None)
    em("tensor", lambda e: e.matmul(PU[:, 32:64], onesr[0:1, :], tinyb[0:1, 16:48],
                                    start=True, stop=True), waits=[], inc=sT)
    t0 = now(sT)
    em("vector", lambda e: e.tensor_copy(flagc[:, :], PU[:, 0:1]), waits=[(sT, t0)], inc=None)
    em("vector", lambda e: e.tensor_copy(Dbc[:, :], PU[:, 32:64]), waits=[], inc=sV)
    v_bc0 = now(sV)
    mm["pt_copy"] = 0
    mm["pu_free"] = v_bc0

    def emit_ln_stats(src_tile, sq_tile, first_waits):
        for g in range(NG):
            for k in range(NK):
                em("tensor", lambda e, k=k, g=g, src_tile=src_tile: e.matmul(
                    PG[0:1, 0:512], onesc[:, :], src_tile(k)[:, g * 512:(g + 1) * 512],
                    start=(k == 0), stop=(k == NK - 1)),
                    waits=(list(first_waits) if (g == 0 and k == 0)
                           else ([(sV, now(sV))] if k == 0 else [])),
                    inc=(sT if k == NK - 1 else None))
            em("vector", lambda e: e.tensor_scalar_mul(
                statr[0:1, 0:512], PG[0:1, 0:512], 1.0 / 2048.0),
                waits=[(sT, now(sT))], inc=sV)
            for k in range(NK):
                em("scalar", lambda e, k=k, g=g, src_tile=src_tile, sq_tile=sq_tile: e.activation(
                    sq_tile(k), src_tile(k)[:, g * 512:(g + 1) * 512], AF.Square),
                    waits=(([(sT, now(sT))] if g > 0 else list(first_waits)) if k == 0 else []),
                    inc=(sC if k == NK - 1 else None))
            for k in range(NK):
                em("tensor", lambda e, k=k, sq_tile=sq_tile: e.matmul(
                    PQ[0:1, 0:512], onesc[:, :], sq_tile(k),
                    start=(k == 0), stop=(k == NK - 1)),
                    waits=([(sC, now(sC))] if k == 0 else []),
                    inc=(sT if k == NK - 1 else None))
            em("vector", lambda e: e.tensor_scalar_mul(
                statr2[0:1, 0:512], PQ[0:1, 0:512], 1.0 / 2048.0),
                waits=[(sT, now(sT))], inc=sV)
            em("vector", lambda e: e.tensor_tensor(
                statr3[0:1, 0:512], statr[0:1, 0:512], statr[0:1, 0:512], ALU.mult),
                waits=[], inc=None)
            em("vector", lambda e: e.tensor_tensor(
                statr2[0:1, 0:512], statr2[0:1, 0:512], statr3[0:1, 0:512], ALU.subtract),
                waits=[], inc=sV)
            em("scalar", lambda e: e.activation(
                statr2[0:1, 0:512], statr2[0:1, 0:512], AF.Ln, bias=epsc[0:1, 1:2]),
                waits=[(sV, now(sV))], inc=None)
            em("scalar", lambda e: e.activation(
                statr2[0:1, 0:512], statr2[0:1, 0:512], AF.Exp, scale=-0.5),
                waits=[], inc=sC)
            em("vector", lambda e: e.tensor_tensor(
                statr3[0:1, 0:512], statr[0:1, 0:512], statr2[0:1, 0:512], ALU.mult),
                waits=[(sC, now(sC))], inc=sV)
            em("sync", lambda e, g=g: e.dma_start(
                out=d_rows[0:1, g * 512:(g + 1) * 512], in_=statr2[0:1, 0:512]),
                waits=[(sC, now(sC))], inc=wrS, amt=16)
            em("sync", lambda e, g=g: e.dma_start(
                out=d_rows[1:2, g * 512:(g + 1) * 512], in_=statr3[0:1, 0:512]),
                waits=[(sV, now(sV))], inc=wrS, amt=16)

    emit_ln_stats(ub, sqb, [dw(dU), (sV, v_bc0)])
    em("gpsimd", lambda e: e.dma_start(out=QbA, in_=d_rows[0:1, :].partition_broadcast(128)),
       waits=[dw(wrS)], inc=dG, amt=16)
    em("gpsimd", lambda e: e.dma_start(out=PbA, in_=d_rows[1:2, :].partition_broadcast(128)),
       waits=[], inc=dG, amt=16)
    q_qbpb = now(dG)
    for k in range(NK):
        em("vector", lambda e, k=k: e.tensor_tensor(ub(k), ub(k), QbA, ALU.mult),
           waits=([(dG, q_qbpb)] if k == 0 else []), inc=None)
        em("vector", lambda e, k=k: e.tensor_tensor(ub(k), ub(k), PbA, ALU.subtract),
           waits=[], inc=(sV if k == NK - 1 else None))
    v_uhat = now(sV)
    em("vector", lambda e: e.memset(xraw(0)[:, 0:3], 0.0), waits=[], inc=None)
    em("vector", lambda e: e.memset(xraw(1)[:, 0:3], 0.0), waits=[], inc=sV)

    wsem = [dW0, dW1]
    wslot = lambda p: wwb[:, (p % 2) * 2048:(p % 2) * 2048 + 2048]
    t_tile = {}
    pb_evac = {0: now(sV), 1: now(sV), 2: now(sV), 3: now(sV)}
    xs_free = {0: [], 1: [], 2: [], 3: []}
    z_free = {0: [], 1: [], 2: [], 3: []}
    raw_free = {0: now(sV), 1: now(sV)}

    def wload(i):
        p = i % 2
        waits = []
        if i >= 2:
            waits.append((sT, t_tile[ETILE_ORDER[i - 2]]))
        em("gpsimd", lambda e, ei=ETILE_ORDER[i], p=p: e.dma_start(out=wslot(p), in_=d_win[ei]),
           waits=waits, inc=wsem[p], amt=16)

    def inproj(i, extra0, evac_fn):
        p = i % 2
        e = ETILE_ORDER[i]
        for g in range(NG):
            pbuf = [PB0, PB1, PG, PQ][g]
            waits = [(sV, pb_evac[g])]
            if g == 0:
                waits.append((wsem[p], 16 * ((i // 2) + 1)))
                waits.extend(extra0)
            for k in range(NK):
                em("tensor", lambda en, p=p, k=k, g=g, pbuf=pbuf: en.matmul(
                    pbuf[:, :], wslot(p)[:, k * 128:(k + 1) * 128],
                    ub(k)[:, g * 512:(g + 1) * 512],
                    start=(k == 0), stop=(k == NK - 1)),
                    waits=(waits if k == 0 else []),
                    inc=(sT if k == NK - 1 else None))
            evac_fn(g, pbuf)
            pb_evac[g] = now(sV)
        t_tile[e] = now(sT)
        if i + 2 < NE:
            wload(i + 2)

    def emit_dt_machinery():
        em("scalar", lambda e: e.activation(rows_dt, rows_tmp, AF.Abs, bias=dtbias[:, 0:1]),
           waits=[(sV, now(sV))], inc=None)
        em("scalar", lambda e: e.activation(rows_dt, rows_dt, AF.Exp, scale=-1.0),
           waits=[], inc=None)
        em("scalar", lambda e: e.activation(rows_dt, rows_dt, AF.Ln, bias=oneb32[:, 0:1]),
           waits=[], inc=None)
        em("scalar", lambda e: e.activation(rows_cum, rows_tmp, AF.Relu, bias=dtbias[:, 0:1]),
           waits=[], inc=sC)
        em("vector", lambda e: e.tensor_tensor(rows_dt, rows_dt, rows_cum, ALU.add),
           waits=[(sC, now(sC))], inc=sV)
        em("vector", lambda e: e.tensor_scalar_mul(rows_dA, rows_dt, Aneg[:, 0:1]),
           waits=[], inc=sV)
        for c in range(NC):
            em("vector", lambda e, c=c: e.tensor_tensor_scan(
                rows_cum[:, c * 256:(c + 1) * 256], rows_dA[:, c * 256:(c + 1) * 256],
                rows_dA[:, c * 256:(c + 1) * 256], 0.0, ALU.add, ALU.bypass),
                waits=[], inc=(sV if c == NC - 1 else None))
        v_cum = now(sV)
        em("sync", lambda e: e.dma_start(out=d_cumd[:, :], in_=rows_cum),
           waits=[(sV, v_cum)], inc=wrS, amt=16)
        for c in range(NC):
            for zt in range(2):
                for bi in range(4):
                    col = c * 256 + zt * 128 + 32 * bi
                    dcol = (c * 2 + zt) * 32
                    em("vector", lambda e, col=col, dcol=dcol, bi=bi: e.transpose(
                        decT[32 * bi:32 * bi + 32, dcol:dcol + 32], rows_dt[:, col:col + 32]),
                        waits=[], inc=None)
                    em("vector", lambda e, col=col, dcol=dcol, bi=bi: e.transpose(
                        ncumT[32 * bi:32 * bi + 32, dcol:dcol + 32], rows_cum[:, col:col + 32]),
                        waits=[], inc=None)
        em("vector", lambda e: e.tensor_copy(dtTb[:, :], decT[:, :]), waits=[], inc=None)
        em("vector", lambda e: e.tensor_scalar_mul(ncumT[:, :], ncumT[:, :], -1.0),
           waits=[], inc=sV)
        em("scalar", lambda e: e.activation(rows_tmp, rows_cum, AF.Exp),
           waits=[(sV, now(sV))], inc=sC)
        em("gpsimd", lambda e: e.dma_start(out=d_ecumd[:, :], in_=rows_tmp),
           waits=[(sC, now(sC))], inc=wrG, amt=16)
        cum_ends = rows_cum.rearrange("p (c s) -> p c s", s=256)[:, :, 255]
        em("scalar", lambda e: e.activation(cdrows[:, 0:8], cum_ends, AF.Exp),
           waits=[(sV, now(sV))], inc=sC)
        em("vector", lambda e: e.tensor_tensor_scan(
            cdrows[:, 16:24], cdrows[:, 0:8], cdrows[:, 0:8], 1.0, ALU.mult, ALU.bypass),
            waits=[(sC, now(sC))], inc=None)
        em("vector", lambda e: e.tensor_scalar_mul(cdrows[:, 17:25], cdrows[:, 16:24], flagc[0:32, 0:1]),
           waits=[], inc=None)
        em("vector", lambda e: e.tensor_copy(cdrows[:, 16:17], flagc[0:32, 0:1]),
           waits=[], inc=None)
        em("vector", lambda e: e.transpose(cdtr[0:32, 0:32], cdrows[:, 0:32]),
           waits=[], inc=None)
        em("vector", lambda e: e.transpose(cdtr[0:32, 32:64], cdrows[:, 16:48]),
           waits=[], inc=sV)
        em("sync", lambda e: e.dma_start(out=d_cdm[:, :], in_=cdtr[:, :]),
           waits=[(sV, now(sV))], inc=wrS, amt=16)
        em("gpsimd", lambda e: e.dma_start(
            out=cdrow0[0:1, 0:256], in_=d_cdm[0:8, 0:32]), waits=[dw(wrS)], inc=dG, amt=16)
        em("gpsimd", lambda e: e.dma_start(
            out=cdrow0[0:1, 256:512], in_=d_cdm[0:8, 32:64]), waits=[], inc=dG, amt=16)
        mm["cdrow_q"] = now(dG)
        for c in range(NC):
            em("vector", lambda e, c=c: e.tensor_scalar_mul(
                rows_tmp[:, c * 256:(c + 1) * 256], rows_tmp[:, c * 256:(c + 1) * 256],
                cdrows[:, 16 + c:17 + c]),
                waits=([dw(wrG)] if c == 0 else []), inc=(sV if c == NC - 1 else None))
        em("gpsimd", lambda e: e.dma_start(out=d_ecumgd[:, :], in_=rows_tmp),
           waits=[(sV, now(sV))], inc=wrG, amt=16)
        for c in range(NC):
            em("vector", lambda e, c=c: e.tensor_scalar_sub(
                rows_tmp[:, c * 256:(c + 1) * 256], rows_cum[:, c * 256:(c + 1) * 256],
                rows_cum[:, c * 256 + 255:c * 256 + 256]),
                waits=([dw(wrG)] if c == 0 else []), inc=(sV if c == NC - 1 else None))
        em("scalar", lambda e: e.activation(rows_tmp, rows_tmp, AF.Exp, scale=-1.0),
           waits=[(sV, now(sV))], inc=sC)
        c_dec = now(sC)
        for c in range(NC):
            for zt in range(2):
                for bi in range(4):
                    col = c * 256 + zt * 128 + 32 * bi
                    dcol = (c * 2 + zt) * 32
                    em("vector", lambda e, col=col, dcol=dcol, bi=bi: e.transpose(
                        decT[32 * bi:32 * bi + 32, dcol:dcol + 32], rows_tmp[:, col:col + 32]),
                        waits=([(sC, c_dec)] if (c == 0 and zt == 0 and bi == 0) else []), inc=None)
        em("vector", lambda e: e.tensor_tensor(decT[:, :], decT[:, :], dtTb[:, :], ALU.mult),
           waits=[], inc=sV)
        mm["dt_done"] = now(sV)

    def emit_cc1_and_fix():
        em("gpsimd", lambda e: e.dma_start(out=d_cc1[:, :], in_=halos[:, :]),
           waits=[(sV, mm["halos_done"])], inc=wrG, amt=16)
        em("gpsimd", lambda e: e.collective_compute(
            "AllGather", ALU.bypass,
            replica_groups=[[0, 1], [2, 3], [4, 5], [6, 7]],
            ins=[d_cc1[:, :]], outs=[d_cc1o[:, :]]),
            waits=[dw(wrG)], inc=sCC)
        em("gpsimd", lambda e: e.dma_start(out=halor[:, :], in_=d_cc1o[0:128, :]),
           waits=[(sCC, now(sCC))], inc=dG, amt=16)
        q_h = now(dG)
        em("vector", lambda e: e.tensor_scalar_mul(halor[:, :], halor[:, :], flagc[:, 0:1]),
           waits=[(dG, q_h)], inc=sV)
        rv = fixraw.rearrange("p (j a) -> p j a", a=6)
        em("vector", lambda e: e.tensor_copy(
            rv[:, :, 0:3], halor[:, :].rearrange("p (j a) -> p j a", a=3)),
            waits=[(sV, now(sV))], inc=None)
        em("vector", lambda e: e.tensor_copy(
            rv[:, :, 3:6], keep3[:, :].rearrange("p (j a) -> p j a", a=3)),
            waits=[], inc=sV)
        v_rv = now(sV)
        cwv = cwb[:, 0:132].rearrange("p (j a) -> p j a", a=4)
        fwv = fixw.rearrange("p (j a) -> p j a", a=3)
        fw2 = fixw2.rearrange("p (j a) -> p j a", a=3)
        em("vector", lambda e: e.tensor_tensor(
            fwv, rv[:, :, 0:3], cwv[:, :, 0:1].broadcast_to([128, 33, 3]), ALU.mult),
            waits=[(sV, v_rv)], inc=None)
        for kk in range(1, 4):
            em("vector", lambda e, kk=kk: e.tensor_tensor(
                fw2, rv[:, :, kk:kk + 3], cwv[:, :, kk:kk + 1].broadcast_to([128, 33, 3]), ALU.mult),
                waits=[], inc=None)
            em("vector", lambda e: e.tensor_tensor(fwv, fwv, fw2, ALU.add), waits=[], inc=None)
        em("vector", lambda e: e.tensor_tensor(
            fwv, fwv, cbb[:, 0:33].unsqueeze(2).broadcast_to([128, 33, 3]), ALU.add),
            waits=[], inc=sV)
        em("scalar", lambda e: e.activation(fixb[:, :], fixw[:, :], AF.Silu),
           waits=[(sV, now(sV))], inc=sC)
        c_fix = now(sC)
        mm["fix_done"] = c_fix
        em("vector", lambda e: e.tensor_copy(bcs[:, 0:3], fixb[:, 96:99]),
           waits=[(sC, c_fix), (sV, mm["bcs_done"])], inc=sV)
        mm["bcs_fixed"] = now(sV)
        em("sync", lambda e: e.dma_start(out=d_csp[:, :], in_=bcs[64:128, :]),
           waits=[(sV, mm["bcs_fixed"])], inc=wrS, amt=16)
        em("gpsimd", lambda e: e.dma_start(out=bcsC[0:64, :], in_=d_csp[:, :]),
           waits=[dw(wrS)], inc=dG, amt=16)
        em("gpsimd", lambda e: e.dma_start(out=bcsC[64:128, :], in_=d_csp[:, :]),
           waits=[], inc=dG, amt=16)
        mm["bcsC_q"] = now(dG)
        for j in range(32):
            em("tensor", lambda e, j=j: e.transpose(
                PT[0:3, 0:128], fixb[:, j * 3:(j + 1) * 3], identb[:, :]),
                waits=([(sC, c_fix)] if j == 0 else []) + [(sV, mm["pt_copy"])],
                inc=sT)
            em("vector", lambda e, j=j: e.tensor_copy(
                fixT[0:3, j * 128:(j + 1) * 128], PT[0:3, 0:128]),
                waits=[(sT, now(sT))], inc=sV)
            mm["pt_copy"] = now(sV)
        mm["fixT_done"] = now(sV)
        em("vector", lambda e: e.memset(carry[:, :], 0.0), waits=[], inc=sV)
        mm["chain_prev"] = now(sV)

    Bdec_slot = lambda gp, i, zt: Bdecb[:, gp * 512 + i * 128 + zt * 64: gp * 512 + i * 128 + zt * 64 + 64]

    x2d = d_xtm[:, :, :].rearrange("j p t -> (j p) t")

    def load_xdtT(c, xv, freekey, extra=()):
        for zt in range(2):
            em("sync", lambda e, c=c, zt=zt, xv=xv: e.dma_start(
                out=xv(zt), in_=x2d[0:4096, (2 * c + zt) * 128:(2 * c + zt) * 128 + 128],
                transpose=True),
                waits=(list(extra) + [dw(wrS), (sV, mm[freekey])]) if zt == 0 else [],
                inc=dXT, amt=16)
        mm[(freekey, "q", c)] = now(dXT)
        if c == 0:
            em("vector", lambda e, xv=xv: e.tensor_copy(xv(0)[0:3, :], fixT[0:3, :]),
               waits=[(dXT, mm[(freekey, "q", c)]), (sV, mm["fixT_done"])], inc=sV)
            mm[(freekey, "fix", 0)] = now(sV)

    def emit_mini_chunk(c):
        if c == 0:
            emit_cc1_and_fix()
            mm["xdtT_m_free"] = now(sV)
        load_xdtT(c, xdtT_m, "xdtT_m_free")
        for zt in range(2):
            em("tensor", lambda e, c=c, zt=zt: e.transpose(
                PT[:, 0:128],
                bcs[:, c * 256 + zt * 128: c * 256 + zt * 128 + 128], identb[:, :]),
                waits=[(sV, mm["bcs_fixed"]), (sV, mm["pt_copy"])], inc=sT)
            mm["pt_t"] = now(sT)
            em("vector", lambda e, c=c, zt=zt: e.tensor_copy(
                BCt[:, (c * 2 + zt) * 64:(c * 2 + zt) * 64 + 64],
                PT[:, 0:64]),
                waits=[(sT, mm["pt_t"])], inc=sV)
            mm["pt_copy"] = now(sV)
        v_bct = now(sV)
        em("scalar", lambda e: e.activation(carryb[:, :], carry[:, :], AF.Copy),
           waits=[(sV, mm["chain_prev"]), (wrG, now(wrG))], inc=sC)
        c_prev = now(sC)
        em("gpsimd", lambda e, c=c: e.dma_start(out=d_prevd[c], in_=carryb[:, :]),
           waits=[(sC, c_prev)], inc=wrG, amt=16)
        em("tensor", lambda e, c=c: e.matmul(
            PU[0:64, 0:32], onesr[0:1, 0:64], cdrow0[0:1, c * 32:(c + 1) * 32],
            start=True, stop=True),
            waits=[(dG, mm["cdrow_q"]), (sV, mm["pu_free"])], inc=sT)
        em("vector", lambda e: e.tensor_copy(cdsb[:, :], PU[0:64, 0:32]),
           waits=[(sT, now(sT))], inc=sV)
        v_cd = now(sV)
        mm["pu_free"] = v_cd
        for g in range(8):
            heads = [4 * g + i for i in range(4)]
            pSt = [PS, PV][g % 2]
            for i, h in enumerate(heads):
                for zt in range(2):
                    blk = c * 2 + zt
                    em("vector", lambda e, h=h, zt=zt, blk=blk, i=i, g=g: e.tensor_scalar_mul(
                        Bdec_slot(g % 2, i, zt), BCt[:, blk * 64:blk * 64 + 64],
                        decT[:, blk * 32 + h:blk * 32 + h + 1]),
                        waits=(([(sV, v_bct)] +
                                ([(sT, mm[("st_mmg", c, g - 2)])] if g >= 2 else
                                 ([(sT, mm[("st_mmg", c - 1, 6 + g)])] if c > 0 else [])))
                               if (i == 0 and zt == 0) else []),
                        inc=(sV if (i == 3 and zt == 1) else None))
            v_bd = now(sV)
            for i, h in enumerate(heads):
                for zt in range(2):
                    em("tensor", lambda e, h=h, zt=zt, i=i, g=g, pSt=pSt: e.matmul(
                        pSt[0:64, i * 128:(i + 1) * 128], Bdec_slot(g % 2, i, zt),
                        xdtT_m(zt)[:, h * 128:(h + 1) * 128],
                        start=(zt == 0), stop=(zt == 1)),
                        waits=(([(sV, v_bd), (dXT, mm[("xdtT_m_free", "q", c)])] +
                                ([(sV, mm[("xdtT_m_free", "fix", 0)])] if c == 0 else []) +
                                ([(sV, mm[("chaing", c, g - 2)])] if g >= 2 else []) +
                                ([(sV, mm[("chaing", c - 1, 6 + g)])] if (c > 0 and g < 2) else []))
                               if (i == 0 and zt == 0) else []),
                        inc=(sT if (i == 3 and zt == 1) else None))
            mm[("st_mmg", c, g)] = now(sT)
            for i, h in enumerate(heads):
                em("vector", lambda e, h=h, i=i, pSt=pSt: e.scalar_tensor_tensor(
                    carry[:, h * 128:(h + 1) * 128], carry[:, h * 128:(h + 1) * 128],
                    cdsb[:, h:h + 1], pSt[0:64, i * 128:(i + 1) * 128], ALU.mult, ALU.add),
                    waits=([(sT, mm[("st_mmg", c, g)]), (sV, v_cd), (sC, c_prev)] if i == 0 else []),
                    inc=(sV if i == 3 else None))
            mm[("chaing", c, g)] = now(sV)
        mm["chain_prev"] = now(sV)
        mm["xdtT_m_free"] = now(sV)
        if c == NC - 1:
            em("scalar", lambda e: e.activation(carryb[:, :], carry[:, :], AF.Copy),
               waits=[(sV, mm["chain_prev"]), (wrG, now(wrG))], inc=sC)
            em("gpsimd", lambda e: e.dma_start(out=d_cc2[:, :], in_=carryb[:, :]),
               waits=[(sC, now(sC))], inc=wrG, amt=16)
            em("gpsimd", lambda e: e.collective_compute(
                "AllGather", ALU.bypass,
                replica_groups=[[0, 1], [2, 3], [4, 5], [6, 7]],
                ins=[d_cc2[:, :]], outs=[d_cc2o[:, :]]),
                waits=[dw(wrG)], inc=sCC)
            mm["cc2_done"] = now(sCC)

    for _k in range(4):
        pb_evac[_k] = now(sV)
    wload(0)
    wload(1)
    minis = 0
    for i, e in enumerate(ETILE_ORDER):
        role = ("dt" if e == 65 else "bc" if e == 64 else "x" if e >= 32 else "z")

        if role == "dt":
            def evac_dt(g, pbuf):
                tmm = now(sT)
                em("vector", lambda en, g=g, pbuf=pbuf: en.tensor_copy(
                    rows_tmp[:, g * 512:(g + 1) * 512], pbuf[0:32, :]),
                    waits=[(sT, tmm)], inc=sV)
            inproj(i, [(sV, v_uhat)], evac_dt)
            emit_dt_machinery()
            continue

        if role in ("bc", "x"):
            xb = i % 4
            dst = xraw(i % 2)
            j = 32 if role == "bc" else (e - 32)

            def evac_x(g, pbuf, dst=dst):
                tmm = now(sT)
                em("vector", lambda en, g=g, pbuf=pbuf, dst=dst: en.tensor_copy(
                    dst[:, 3 + g * 512: 3 + (g + 1) * 512], pbuf[:, :]),
                    waits=[(sT, tmm)], inc=sV)
            inproj(i, [(sV, raw_free[i % 2])], evac_x)
            em("vector", lambda en, dst=dst, j=j: en.tensor_copy(
                keep3[:, j * 3:(j + 1) * 3], dst[:, 3:6]), waits=[], inc=None)
            em("vector", lambda en, dst=dst, j=j: en.tensor_copy(
                halos[:, j * 3:(j + 1) * 3], dst[:, 2048:2051]), waits=[], inc=sV)
            if j == 31:
                mm["halos_done"] = now(sV)
            xs = xsst(xb)
            em("vector", lambda en, dst=dst, j=j, xs=xs: en.tensor_scalar(
                xs, dst[:, 0:2048], cwb[:, 4 * j:4 * j + 1], cbb[:, j:j + 1],
                ALU.mult, ALU.add), waits=list(xs_free[xb]), inc=None)
            for kk in range(1, 4):
                em("vector", lambda en, dst=dst, j=j, xs=xs, kk=kk: en.scalar_tensor_tensor(
                    xs, dst[:, kk:kk + 2048], cwb[:, 4 * j + kk:4 * j + kk + 1], xs,
                    ALU.mult, ALU.add), waits=[], inc=(sV if kk == 3 else None))
            raw_free[i % 2] = now(sV)
            em("scalar", lambda en, xs=xs: en.activation(xs, xs, AF.Silu),
               waits=[(sV, now(sV))], inc=sC)
            c_silu = now(sC)
            if role == "bc":
                em("vector", lambda en, xs=xs: en.tensor_copy(bcs[:, :], xs),
                   waits=[(sC, c_silu)], inc=sV)
                mm["bcs_done"] = now(sV)
                xs_free[xb] = [(sV, now(sV))]
            else:
                em("sync", lambda en, xs=xs, j=j: en.dma_start(out=d_xtm[j], in_=xs),
                   waits=[(sC, c_silu)], inc=wrS, amt=16)
                for c in range(NC):
                    em("scalar", lambda en, xs=xs, j=j, c=c: en.dma_start(
                        out=d_xcm[c][:, j, :], in_=xs[:, c * 256:(c + 1) * 256]),
                        waits=[], inc=wrA, amt=16)
                xs_free[xb] = [dw(wrS), dw(wrA)]
            continue

        zb = i % 4
        zi = e
        dstz = ztst(zb)

        def evac_z(g, pbuf, dstz=dstz):
            tmm = now(sT)
            em("vector", lambda en, g=g, pbuf=pbuf, dstz=dstz: en.tensor_copy(
                dstz[:, g * 512:(g + 1) * 512], pbuf[:, :]),
                waits=[(sT, tmm)], inc=sV)
        inproj(i, list(z_free[zb]), evac_z)
        em("scalar", lambda en, dstz=dstz: en.activation(dstz, dstz, AF.Silu),
           waits=[(sV, now(sV))], inc=sC)
        c_zs = now(sC)
        for c in range(NC):
            em("scalar", lambda en, dstz=dstz, zi=zi, c=c: en.dma_start(
                out=d_zcm[c][:, zi, :], in_=dstz[:, c * 256:(c + 1) * 256]),
                waits=[], inc=wrA, amt=16)
        z_free[zb] = [dw(wrA)]
        if zi % 4 == 3:
            emit_mini_chunk(minis)
            minis += 1
    assert minis == NC

    ckq = {}
    pvq = {}
    ebq = {}
    ecq = {}
    mdone = {}

    def load_chunk(c, extra=()):
        em("scalar", lambda e, c=c: e.dma_start(out=xck(c), in_=d_xcm[c][:, :, :]),
           waits=list(extra) +
                 ([(sV, mm[("evg", c - 2, 7)]), (sT, mm[("ymmg", c - 2, 7)])] if c >= 2 else []) +
                 [dw(wrA)],
           inc=dCK, amt=16)
        em("scalar", lambda e, c=c: e.dma_start(out=zck(c), in_=d_zcm[c][:, :, :]),
           waits=[], inc=dCK, amt=16)
        ckq[c] = now(dCK)
        em("gpsimd", lambda e, c=c: e.dma_start(out=prevb(c), in_=d_prevd[c]),
           waits=[dw(wrG)] + ([(sT, mm[("ymmg", c - 2, 7)])] if c >= 2 else []),
           inc=dG, amt=16)
        pvq[c] = now(dG)

    def load_EBc(c, qq, wait_marks):
        em("sync", lambda e, c=c, qq=qq: e.dma_start(
            out=Ebc(c, qq),
            in_=d_cumd[qq * 8:(qq + 1) * 8, c * 256:(c + 1) * 256].partition_broadcast(128)),
            waits=wait_marks + [dw(wrS)], inc=dEB, amt=16)
        ebq[(c, qq)] = now(dEB)

    def load_EC(c, qq, wait_marks):
        em("sync", lambda e, c=c, qq=qq: e.dma_start(
            out=ecumb(qq),
            in_=d_ecumd[qq * 8:(qq + 1) * 8, c * 256:(c + 1) * 256].partition_broadcast(64)),
            waits=list(wait_marks), inc=dEC, amt=16)
        em("sync", lambda e, c=c, qq=qq: e.dma_start(
            out=ecumbG(qq),
            in_=d_ecumgd[qq * 8:(qq + 1) * 8, c * 256:(c + 1) * 256].partition_broadcast(64)),
            waits=[], inc=dEC, amt=16)
        ecq[(c, qq)] = now(dEC)

    xtq = {}

    def load_xT(c, extra=()):
        for zt in range(2):
            em("scalar", lambda e, c=c, zt=zt: e.dma_start(
                out=xdtT_b(zt), in_=x2d[0:4096, (2 * c + zt) * 128:(2 * c + zt) * 128 + 128],
                transpose=True),
                waits=((list(extra) +
                        ([(sT, mm[("ymmg", c - 1, 7)])] if c > 0 else []))
                       if zt == 0 else []),
                inc=dXT, amt=16)
        xtq[c] = now(dXT)
        if c == 0:
            em("vector", lambda e: e.tensor_copy(xdtT_b(0)[0:3, :], fixT[0:3, :]),
               waits=[(dXT, xtq[0]), (sV, mm["fixT_done"])], inc=sV)
            mm["fix0_done"] = now(sV)

    b_gate = [(sT, now(sT)), (sV, now(sV)), (sC, now(sC)), dw(wrG)]
    em("gpsimd", lambda e: e.dma_start(out=BIG1[64:128, 12288:16384], in_=d_cc2o[0:64, :]),
       waits=[(sT, now(sT)), (sCC, mm["cc2_done"])], inc=dG, amt=16)
    em("gpsimd", lambda e: e.dma_start(out=BIG1[64:128, 16384:20480], in_=d_cc2o[0:64, :]),
       waits=[], inc=dG, amt=16)
    mm["crecv_q2"] = now(dG)
    DIall = BIG1[:, 20480:24576]
    for h in range(NH):
        em("vector", lambda e, h=h: e.tensor_scalar_mul(
            DIall[:, h * 128:(h + 1) * 128], identb[:, :], Dbc[:, h:h + 1]),
            waits=[], inc=(sV if h == NH - 1 else None))
    mm["di_done"] = now(sV)
    load_xT(0, extra=list(b_gate))
    load_chunk(0, extra=[(sT, now(sT))])
    em("vector", lambda e: e.tensor_copy(
        xck(0).rearrange("p (j t) -> p j t", t=256)[:, :, 0:3],
        fixb[:, 0:96].rearrange("p (j a) -> p j a", a=3)),
        waits=[(dCK, ckq[0]), (sC, mm["fix_done"])], inc=sV)
    load_chunk(1)
    load_EBc(0, 0, list(b_gate))
    load_EC(0, 0, list(b_gate))
    load_EC(0, 1, list(b_gate))
    b_gate2 = [(sT, now(sT)), (sV, now(sV)), (sC, now(sC)), dw(wrG)]
    load_EBc(0, 1, list(b_gate2))
    load_EBc(0, 2, list(b_gate2))

    def pre_gt(c):
        for zt in range(2):
            em("tensor", lambda e, c=c, zt=zt: e.matmul(
                PU[:, zt * 256:(zt + 1) * 256],
                bcs[0:64, c * 256 + zt * 128: c * 256 + zt * 128 + 128],
                bcsC[0:64, c * 256:(c + 1) * 256], start=True, stop=True),
                waits=(([(sV, mm["bcs_fixed"]), (dG, mm["bcsC_q"]), (sV, mm["pu_free"])] +
                        list(b_gate)
                        if c == 0 else [(sV, mm[("gt_ev", c - 1)])]) if zt == 0 else []),
                inc=sT)
        t_gt = now(sT)
        for zt in range(2):
            em("vector", lambda e, c=c, zt=zt: e.tensor_tensor(
                GtsS(c)[:, zt * 256:(zt + 1) * 256], PU[:, zt * 256:(zt + 1) * 256],
                trilm[:, zt * 256:(zt + 1) * 256], ALU.mult),
                waits=([(sT, t_gt)] if zt == 0 else []),
                inc=(sV if zt == 1 else None))
        mm[("gt_ev", c)] = now(sV)

    def stage1(G):
        c, g4 = divmod(G, 8)
        qq = g4 // 2
        for i in range(4):
            h = 4 * g4 + i
            for zt in range(2):
                blk = (c * 2 + zt) * 32
                em("vector", lambda e, G=G, h=h, i=i, zt=zt, blk=blk, qq=qq, c=c: e.tensor_scalar(
                    MhH(G, i)[:, zt * 256:(zt + 1) * 256],
                    Ebc(c, qq)[:, ((h % 8) * 256):((h % 8) * 256 + 256)],
                    ncumT[:, blk + h:blk + h + 1], 0.0, ALU.add, ALU.min),
                    waits=(([(dEB, ebq[(c, qq)])] +
                            ([(sT, mm[("ymmG", G - 3)])] if G >= 3 else []) +
                            (list(b_gate2) if G <= 2 else []))
                           if (i == 0 and zt == 0) else []),
                    inc=(sV if (i == 3 and zt == 1) else None))
        mm[("v_e", G)] = now(sV)
        em("scalar", lambda e, G=G: e.activation(MhS(G)[:, :], MhS(G)[:, :], AF.Exp),
           waits=[(sV, mm[("v_e", G)])], inc=sC)
        mm[("c_exp", G)] = now(sC)

    def stage2(G):
        c, g4 = divmod(G, 8)
        qq = g4 // 2
        gp = g4 % 2
        bank = [(PB0, PB1), (PG, PQ)][gp]
        em("vector", lambda e, c=c, gp=gp, qq=qq: e.tensor_tensor(
            Cxp[:, gp * 1024:gp * 1024 + 1024].rearrange("p (i t) -> p i t", t=256),
            bcsC[:, c * 256:(c + 1) * 256].unsqueeze(1).broadcast_to([128, 4, 256]),
            ecumbF(qq)[:, gp * 1024:gp * 1024 + 1024].rearrange("p (i t) -> p i t", t=256),
            ALU.mult),
            waits=[(dEC, ecq[(c, qq)]), (sV, mm["di_done"])] +
                  ([(sT, mm[("ymmG", G - 2)])] if G >= 2 else []),
            inc=sV)
        mm[("v_aux", G)] = now(sV)
        for i in range(4):
            h = 4 * g4 + i
            for zt in range(2):
                blk = (c * 2 + zt) * 32
                em("vector", lambda e, G=G, i=i, h=h, zt=zt, blk=blk, c=c: e.scalar_tensor_tensor(
                    MhH(G, i)[:, zt * 256:(zt + 1) * 256],
                    MhH(G, i)[:, zt * 256:(zt + 1) * 256],
                    dtTb[:, blk + h:blk + h + 1],
                    GtsS(c)[:, zt * 256:(zt + 1) * 256],
                    ALU.mult, ALU.mult),
                    waits=([(sC, mm[("c_exp", G)])] if (i == 0 and zt == 0) else []),
                    inc=(sV if (i == 3 and zt == 1) else None))
        mm[("v_m", G)] = now(sV)
        for i in range(4):
            h = 4 * g4 + i
            pY = bank[i // 2][:, (i % 2) * 256:(i % 2) * 256 + 256]
            em("tensor", lambda e, G=G, h=h, i=i, pY=pY, c=c: e.matmul(
                pY, xdtT_b(0)[:, h * 128:(h + 1) * 128], MhH(G, i)[:, 0:256],
                start=True, stop=False),
                waits=(([(sV, mm[("v_m", G)]), (sV, mm[("v_aux", G)]), (dCK, ckq[c]),
                         (dXT, xtq[c])] +
                        ([(sV, mm[("evG", G - 2)])] if G >= 2 else []) +
                        ([(sV, mm["fix0_done"])] if c == 0 and g4 == 0 and i == 0 else []) +
                        ([(dG, mm["crecv_q2"]), (dG, pvq[c])] if g4 == 0 else []))
                       if i == 0 else []),
                inc=None)
            em("tensor", lambda e, G=G, h=h, i=i, pY=pY: e.matmul(
                pY, xdtT_b(1)[:, h * 128:(h + 1) * 128], MhH(G, i)[:, 256:512],
                start=False, stop=False), waits=[], inc=None)
            em("tensor", lambda e, c=c, h=h, pY=pY: e.matmul(
                pY, DIall[:, h * 128:(h + 1) * 128],
                xck(c)[:, h * 256:(h + 1) * 256], start=False, stop=False),
                waits=[], inc=None)
            em("tensor", lambda e, c=c, h=h, i=i, gp=gp, pY=pY: e.matmul(
                pY, prevbF(c)[:, h * 128:(h + 1) * 128],
                Cxp[:, gp * 1024 + i * 256:gp * 1024 + (i + 1) * 256],
                start=False, stop=True),
                waits=[],
                inc=(sT if i == 3 else None))
        mm[("ymmG", G)] = now(sT)
        mm[("ymmg", c, g4)] = now(sT)

    def stage3(G):
        c, g4 = divmod(G, 8)
        bank = [(PB0, PB1), (PG, PQ)][g4 % 2]
        for pr in range(2):
            h0 = 4 * g4 + 2 * pr
            em("vector", lambda e, c=c, h0=h0, bk=bank[pr]: e.tensor_tensor(
                BIG1[:, 2048 + (h0 % 4) * 256: 2048 + (h0 % 4) * 256 + 512],
                bk[:, 0:512], zck(c)[:, h0 * 256:(h0 + 2) * 256], ALU.mult),
                waits=([(sT, mm[("ymmG", G)])] if pr == 0 else []), inc=None)
        for pr in range(2):
            h0 = 4 * g4 + 2 * pr
            em("vector", lambda e, h0=h0: e.tensor_tensor(
                BIG1[:, 3072 + (h0 % 4) * 256: 3072 + (h0 % 4) * 256 + 512],
                BIG1[:, 2048 + (h0 % 4) * 256: 2048 + (h0 % 4) * 256 + 512],
                BIG1[:, 2048 + (h0 % 4) * 256: 2048 + (h0 % 4) * 256 + 512], ALU.mult),
               waits=[], inc=None)
        for i in range(4):
            h = 4 * g4 + i
            em("vector", lambda e, h=h: e.tensor_scalar_mul(ynt(h), ygt(h), nwb[:, h:h + 1]),
               waits=([(wrG, mm.get(("ynwG", G - 2), 0))] if i == 0 else []),
               inc=(sV if i == 3 else None))
        mm[("evG", G)] = now(sV)
        mm[("evg", c, g4)] = now(sV)
        for i2 in range(4):
            h2 = 4 * g4 + i2
            em("tensor", lambda e, h2=h2: e.matmul(
                PS[0:1, 0:256], onesc[:, :], sqt(h2),
                start=(h2 == 0), stop=(h2 == NH - 1)),
                waits=([(sV, mm[("evG", G)])] if i2 == 0 else []) +
                      ([(sC, mm[("rinv", c - 1)])] if (h2 == 0 and c > 0) else []),
                inc=(sT if h2 == NH - 1 else None))
        if g4 == 7:
            mm[("t_rms", c)] = now(sT)
        em("gpsimd", lambda e, c=c, g4=g4: e.dma_start(
            out=d_yns[c][:, 4 * g4:4 * g4 + 4, :], in_=yntg(g4)),
            waits=[(sV, mm[("evG", G)]), (wrG, mm.get(("ynwG", G - 2), 0))], inc=wrG, amt=16)
        mm[("ynwG", G)] = now(wrG)

    def rinv_ep(c):
        em("scalar", lambda e, c=c: e.activation(
            rinvr[0:1, 0:256], PS[0:1, 0:256], AF.Ln,
            bias=epsc[0:1, 1:2], scale=1.0 / 4096.0),
            waits=[(sT, mm[("t_rms", c)])] + ([dw(wrS)] if c > 0 else []), inc=None)
        em("scalar", lambda e, c=c: e.activation(
            rinvr[0:1, 0:256], rinvr[0:1, 0:256],
            AF.Exp, scale=-0.5), waits=[], inc=sC)
        mm[("rinv", c)] = now(sC)
        em("sync", lambda e, c=c: e.dma_start(
            out=d_rows[2:3, c * 256:(c + 1) * 256], in_=rinvr[0:1, 0:256]),
            waits=[(sC, mm[("rinv", c)])], inc=wrS, amt=16)

    for G in range(66):
        c, g4 = divmod(G, 8)
        if G < 64:
            if g4 == 0:
                pre_gt(c)
            stage1(G)
        if 1 <= G <= 64:
            Gp = G - 1
            stage2(Gp)
            cp, gq = divmod(Gp, 8)
            if gq == 7 and cp + 1 < NC:
                load_xT(cp + 1)
            if gq % 2 == 1:
                nq = gq // 2 + 3
                if nq < 4:
                    load_EBc(cp, nq, [(sV, mm[("v_e", Gp)])])
                elif cp + 1 < NC:
                    load_EBc(cp + 1, nq - 4, [(sV, mm[("v_e", Gp)])])
                nq2 = gq // 2 + 2
                if nq2 < 4:
                    load_EC(cp, nq2, [(sV, mm[("v_aux", Gp)])])
                elif cp + 1 < NC:
                    load_EC(cp + 1, nq2 - 4, [(sV, mm[("v_aux", Gp)])])
        if G >= 2:
            Gp = G - 2
            stage3(Gp)
            cp, gq = divmod(Gp, 8)
            if gq == 7:
                rinv_ep(cp)
                if cp + 2 < NC:
                    load_chunk(cp + 2)

    em("gpsimd", lambda e: e.dma_start(out=rinvb, in_=d_rows[2:3, :].partition_broadcast(128)),
       waits=[dw(wrS)], inc=dG, amt=16)
    q_rinv = now(dG)

    woq = {}

    def load_wout(H, dd):
        par = dd % 2
        em("gpsimd", lambda e, dd=dd, par=par: e.dma_start(
            out=wwb[:, par * 4096:(par + 1) * 4096], in_=d_wout[dd]),
            waits=([(sT, mm[("omm", H, dd - 2)])] if dd >= 2 else
                   ([(sT, mm[("omm", H - 1, 14 + dd)])] if H > 0 else [])),
            inc=dWO, amt=16)
        woq[(H, dd)] = now(dWO)

    cev = {}
    for H in range(2):
        for j in range(32):
            for cc in range(4):
                em("sync", lambda e, H=H, j=j, cc=cc: e.dma_start(
                    out=ynb(j)[:, cc * 256:(cc + 1) * 256], in_=d_yns[4 * H + cc][:, j, :]),
                    waits=([dw(wrA), (sV, mm[("evg", NC - 1, 7)])] if (j == 0 and cc == 0 and H == 0)
                           else ([(sT, mm[("omm", 0, 15)])] if (j == 0 and cc == 0 and H == 1) else [])),
                    inc=dYN, amt=16)
        q_yn = now(dYN)
        if H == 0:
            load_wout(0, 0)
            load_wout(0, 1)
        for dd in range(16):
            par = dd % 2
            em("sync", lambda e, dd=dd, H=H, par=par: e.dma_start(
                out=ureb(par), in_=d_uT[dd * 128:(dd + 1) * 128, H * 1024:(H + 1) * 1024]),
                waits=([(sV, cev[(H, dd - 2)])] if dd >= 2 else
                       ([(sV, cev[(H - 1, 14 + dd)])] if H > 0 else [])),
                inc=dUR, amt=16)
            q_ur = now(dUR)
            for g in range(2):
                pbuf = [PB0, PB1, PG, PQ][(dd * 2 + g) % 4]
                for k in range(32):
                    em("tensor", lambda e, dd=dd, H=H, g=g, k=k, par=par, pbuf=pbuf: e.matmul(
                        pbuf[:, :], wwb[:, par * 4096 + k * 128: par * 4096 + (k + 1) * 128],
                        ynb(k)[:, g * 512:(g + 1) * 512],
                        start=(k == 0), stop=(k == 31)),
                        waits=([(dWO, woq[(H, dd)]), (dYN, q_yn)] +
                               ([(sV, cev[(H, dd - 2)])] if dd >= 2 else
                                ([(sV, cev[(H - 1, 14 + dd)])] if H > 0 else []))
                               if k == 0 else []),
                        inc=(sT if k == 31 else None))
                t_omm = now(sT)
                if g == 1:
                    mm[("omm", H, dd)] = t_omm
                em("vector", lambda e, H=H, g=g, pbuf=pbuf: e.tensor_tensor(
                    x1t(g), pbuf[:, :], rinvb[:, H * 1024 + g * 512: H * 1024 + (g + 1) * 512],
                    ALU.mult), waits=[(sT, t_omm), (dG, q_rinv), (wrA, now(wrA))], inc=None)
                em("vector", lambda e, g=g, par=par: e.tensor_tensor(
                    x1t(g), x1t(g), ureb(par)[:, g * 512:(g + 1) * 512], ALU.add),
                    waits=[(dUR, q_ur)], inc=None)
                em("vector", lambda e, dd=dd, H=H, g=g: e.tensor_copy(
                    x1b(dd)[:, H * 1024 + g * 512: H * 1024 + (g + 1) * 512], x1t(g)),
                    waits=[], inc=sV)
                em("scalar", lambda e, dd=dd, H=H, g=g: e.dma_start(
                    out=d_x1[dd][:, H * 1024 + g * 512: H * 1024 + (g + 1) * 512], in_=x1t(g)),
                    waits=[(sV, now(sV))], inc=wrA, amt=16)
            cev[(H, dd)] = now(sV)
            if dd + 2 < 16:
                load_wout(H, dd + 2)
            elif H == 0 and dd == 14:
                load_wout(1, 0)
            elif H == 0 and dd == 15:
                load_wout(1, 1)
    v_x1done = now(sV)
    t_c_end = now(sT)

    emit_ln_stats(x1b, sqd, [(sV, v_x1done)])
    em("gpsimd", lambda e: e.dma_start(out=QbD, in_=d_rows[0:1, :].partition_broadcast(128)),
       waits=[dw(wrS)], inc=dG, amt=16)
    em("gpsimd", lambda e: e.dma_start(out=PbD, in_=d_rows[1:2, :].partition_broadcast(128)),
       waits=[], inc=dG, amt=16)
    q_qp2 = now(dG)
    for k in range(NK):
        em("vector", lambda e, k=k: e.tensor_tensor(x1b(k), x1b(k), QbD, ALU.mult),
           waits=([(dG, q_qp2)] if k == 0 else []), inc=None)
        em("vector", lambda e, k=k: e.tensor_tensor(x1b(k), x1b(k), PbD, ALU.subtract),
           waits=[], inc=(sV if k == NK - 1 else None))
    v_xh = now(sV)

    for o in range(16):
        em("gpsimd", lambda e, o=o: e.dma_start(out=swb(o), in_=d_wffn[o]),
           waits=([(sT, t_c_end)] if o == 0 else []), inc=dWO, amt=16)
        q_w = now(dWO)
        em("scalar", lambda e, o=o: e.activation(abst, swb(o), AF.Abs),
           waits=[(dWO, q_w)] + ([(sT, now(sT))] if o > 0 else [(sV, v_x1done)]), inc=None)
        em("scalar", lambda e, o=o: e.sign(swb(o), swb(o)), waits=[], inc=sC)
        c_pr = now(sC)
        for k in range(NK):
            em("tensor", lambda e, o=o, k=k: e.matmul(
                PV[:, 0:1], abst[:, k * 128:(k + 1) * 128], onesc[:, :],
                start=(k == 0), stop=(k == NK - 1)),
                waits=([(sC, c_pr), (sV, now(sV))] if k == 0 else []),
                inc=(sT if k == NK - 1 else None))
        em("vector", lambda e, o=o: e.tensor_scalar_mul(alf[:, o:o + 1], PV[:, 0:1], 1.0 / 2048.0),
           waits=[(sT, now(sT))], inc=sV)
    v_alf = now(sV)

    x1q = {}

    def load_x1r(o):
        par = o % 2
        em("gpsimd", lambda e, o=o, par=par: e.dma_start(out=x1rb(par), in_=d_x1[o]),
           waits=([dw(wrA), (sV, v_xh)] if o < 2 else [(sV, mm[("fev", o - 2, NG - 1)])]),
           inc=dG, amt=16)
        x1q[o] = now(dG)

    load_x1r(0)
    load_x1r(1)
    for o in range(16):
        par = o % 2
        for g in range(NG):
            pbuf = [PB0, PB1, PG, PQ][g]
            for k in range(NK):
                em("tensor", lambda e, o=o, g=g, k=k, pbuf=pbuf: e.matmul(
                    pbuf[:, :], swb(o)[:, k * 128:(k + 1) * 128],
                    x1b(k)[:, g * 512:(g + 1) * 512],
                    start=(k == 0), stop=(k == NK - 1)),
                    waits=(([(sV, v_xh), (sV, v_alf)] +
                            ([(sV, mm[("fev", o - 1, g)])] if o > 0 else []))
                           if k == 0 else []),
                    inc=(sT if k == NK - 1 else None))
            t_f = now(sT)
            em("vector", lambda e, o=o, g=g, pbuf=pbuf: e.tensor_scalar(
                d_ot(g), pbuf[:, :], alf[:, o:o + 1], fbb[:, o:o + 1], ALU.mult, ALU.add),
                waits=[(sT, t_f), (outw, now(outw))] +
                      ([(sV, mm[("fev", o, g - 2)])] if g >= 2 else
                       ([(sV, mm[("fev", o - 1, g + 2)])] if o > 0 else [])),
                inc=None)
            em("vector", lambda e, o=o, g=g, par=par: e.tensor_tensor(
                d_ot(g), d_ot(g), x1rb(par)[:, g * 512:(g + 1) * 512], ALU.add),
                waits=[(dG, x1q[o])], inc=sV)
            mm[("fev", o, g)] = now(sV)
            em("scalar", lambda e, o=o, g=g: e.dma_start(
                out=d_out[o * 128:(o + 1) * 128, g * 512:(g + 1) * 512], in_=d_ot(g)),
                waits=[(sV, mm[("fev", o, g)])], inc=outw, amt=16)
        if o + 2 < 16:
            load_x1r(o + 2)

    final_outw = now(outw)
    assert final_outw == 16 * 64

    def replay(name, eng):
        for (e, fn, waits, inc, amt) in steps:
            if e != name:
                continue
            for (s, v) in waits:
                eng.wait_ge(s, v)
            inst = fn(eng)
            if inc is not None:
                inst.then_inc(inc, amt)

    with nc.Block() as block:
        @block.sync
        def _(e):
            replay("sync", e)
            e.wait_ge(outw, final_outw)

        @block.tensor
        def _(e):
            replay("tensor", e)

        @block.vector
        def _(e):
            replay("vector", e)

        @block.scalar
        def _(e):
            replay("scalar", e)

        @block.gpsimd
        def _(e):
            replay("gpsimd", e)

    return nc


def kernel(**inputs):
    u = np.asarray(inputs["L0_attn_u"], dtype=np.float32)
    Win = np.asarray(inputs["L0_attn_W_in"], dtype=np.float32)
    convw = np.asarray(inputs["L0_attn_conv_w"], dtype=np.float32)
    convb = np.asarray(inputs["L0_attn_conv_b"], dtype=np.float32)
    dtb = np.asarray(inputs["L0_attn_dt_bias"], dtype=np.float32)
    alog = np.asarray(inputs["L0_attn_A_log"], dtype=np.float32)
    dv = np.asarray(inputs["L0_attn_D"], dtype=np.float32)
    nw = np.asarray(inputs["L0_attn_norm_w"], dtype=np.float32)
    wout = np.asarray(inputs["L0_attn_W_out"], dtype=np.float32)
    wffn = np.asarray(inputs["L0_ffn_weight"], dtype=np.float32)
    fbias = np.asarray(inputs["L0_ffn_bias"], dtype=np.float32)

    winp = np.zeros((NE * 128, D), dtype=np.float32)
    winp[:8352] = Win
    w_in_st = np.ascontiguousarray(winp.reshape(NE, 128, NK, 128).transpose(0, 3, 2, 1))
    w_out_st = np.ascontiguousarray(wout.reshape(16, 128, 32, 128).transpose(0, 3, 2, 1))
    w_ffn_st = np.ascontiguousarray(wffn.reshape(16, 128, 16, 128).transpose(0, 3, 2, 1))
    cw_st = np.ascontiguousarray(convw.reshape(33, 128, 4).transpose(1, 0, 2))
    cb_st = np.ascontiguousarray(convb.reshape(33, 128).T)
    nw_st = np.ascontiguousarray(nw.reshape(32, 128).T)
    fb_st = np.ascontiguousarray(fbias.reshape(16, 128).T)
    tril = np.zeros((2, 128, 256), dtype=np.float32)
    for zt in range(2):
        for zp in range(128):
            tril[zt, zp, zt * 128 + zp:] = 1.0
    base = {
        "w_in": w_in_st, "w_out": w_out_st, "w_ffn": w_ffn_st,
        "conv_w": cw_st, "conv_b": cb_st,
        "dt_bias": dtb.reshape(32, 1), "A_log": alog.reshape(32, 1),
        "Dvec": dv.reshape(1, 32), "norm_w": nw_st, "ffn_bias": fb_st,
        "ones_r": np.ones((1, 128), dtype=np.float32),
        "ones_c": np.ones((128, 1), dtype=np.float32),
        "ident": np.eye(128, dtype=np.float32), "trilm_d": tril,
    }
    in_maps = []
    for core in range(8):
        b, hf = core // 2, core % 2
        m = dict(base)
        m["uT"] = np.ascontiguousarray(u[b, hf * T:(hf + 1) * T, :].T)
        m["flagv"] = np.array([[float(hf), EPS, 1.0, 0.0]], dtype=np.float32)
        in_maps.append(m)

    nc = _build()
    res = run_bass_kernel_spmd(nc, in_maps, core_ids=list(range(8)))
    out = np.empty((4, 4096, D), dtype=np.float32)
    for core in range(8):
        b, hf = core // 2, core % 2
        out[b, hf * T:(hf + 1) * T, :] = res.results[core]["out_dT"].T
    return out
